# version 06
# baseline (speedup 1.0000x reference)
import contextlib
import numpy as np
import ml_dtypes

import concourse.bass as bass
import concourse.mybir as mybir
from concourse.bass_utils import run_bass_kernel_spmd

F32 = mybir.dt.float32
BF16 = mybir.dt.bfloat16
ALU = mybir.AluOpType
AF = mybir.ActivationFunctionType
AX = mybir.AxisListType

B, T, C = 4, 8192, 1024
H = 16
NC_ = 8
TOWN = 4096
TPRE = 4096
CH = 512
NT = CH // 128
NCH_OWN = TOWN // CH
NCH_PRE = TPRE // CH
MAGIC = 12582912.0
STOP_AFTER = None


class Fence:

    def __init__(self, nc):
        self.nc = nc
        self.es = contextlib.ExitStack()
        self.sem = {}
        self.cnt = {}
        self.last = {}
        for name in ("T", "V", "S", "LD", "ST", "GD"):
            self.sem[name] = self.es.enter_context(nc.semaphore("sem" + name))
            self.cnt[name] = 0
            self.last[name] = None
        self.eng = {"T": nc.tensor, "V": nc.vector, "S": nc.scalar,
                    "G": nc.gpsimd, "Y": nc.sync}

    def put(self, name, instr):
        self.last[name] = instr
        return instr

    tflush_cb = None
    twritten = None

    def dma(self, name, instr):
        instr.then_inc(self.sem[name], 16)
        self.cnt[name] += 16
        return instr

    def flush(self, name):
        if self.last[name] is not None:
            if name == "T" and self.tflush_cb is not None:
                n = self.tflush_cb()
                if n:
                    self.cnt[name] += n
                    self.last[name] = None
                    return
            self.last[name].then_inc(self.sem[name], 1)
            self.cnt[name] += 1
            self.last[name] = None

    def fence(self, prods, cons):
        for p in prods:
            self.flush(p)
        for c in cons:
            for p in prods:
                if self.cnt[p] > 0:
                    self.eng[c].wait_ge(self.sem[p], self.cnt[p])


def build_program():
    nc = bass.Bass("TRN2", target_bir_lowering=False, debug=False,
                   detect_race_conditions=False)

    d_xpre = nc.dram_tensor("xpre", [TPRE, C], F32, kind="ExternalInput").ap()
    d_xown = nc.dram_tensor("xown", [TOWN, C], F32, kind="ExternalInput").ap()
    d_wqkv = nc.dram_tensor("wqkv", [C, 3 * C], F32, kind="ExternalInput").ap()
    d_wout = nc.dram_tensor("wout", [C, C], F32, kind="ExternalInput").ap()
    d_rhsT = nc.dram_tensor("rhsT", [C, C], F32, kind="ExternalInput").ap()
    d_bqkv_cm = nc.dram_tensor("bqkv_cm", [128, 16], F32, kind="ExternalInput").ap()
    d_bv = nc.dram_tensor("bv_bcast", [128, C], F32, kind="ExternalInput").ap()
    d_bout = nc.dram_tensor("bout_bcast", [128, C], F32, kind="ExternalInput").ap()
    d_omega = nc.dram_tensor("omega_h", [128, 1024], F32, kind="ExternalInput").ap()
    d_qn = nc.dram_tensor("quad_n", [1, 2], F32, kind="ExternalInput").ap()
    d_qw = nc.dram_tensor("quad_w", [1, 2], F32, kind="ExternalInput").ap()
    d_gate = nc.dram_tensor("gate", [128, 1], F32, kind="ExternalInput").ap()
    d_idnb = nc.dram_tensor("idn_bf", [128, 128], BF16, kind="ExternalInput").ap()
    d_idnd = nc.dram_tensor("idn_dup", [128, 64], BF16, kind="ExternalInput").ap()
    d_idnf = nc.dram_tensor("idn_f32", [128, 128], F32, kind="ExternalInput").ap()
    d_tri = nc.dram_tensor("trimask", [128, 128], BF16, kind="ExternalInput").ap()
    d_ones = nc.dram_tensor("ones_col", [128, 1], BF16, kind="ExternalInput").ap()
    d_eyer = nc.dram_tensor("eye_rows", [1, 1024], F32, kind="ExternalInput").ap()
    d_selpT = nc.dram_tensor("selpatT", [2, 128], F32, kind="ExternalInput").ap()
    d_selp2 = nc.dram_tensor("selpat2", [2, 64], F32, kind="ExternalInput").ap()
    d_y = nc.dram_tensor("y", [TOWN, C], F32, kind="ExternalOutput").ap()

    es = contextlib.ExitStack()
    SB = lambda n, sh, dt=F32: es.enter_context(nc.sbuf_tensor(n, sh, dt))
    PS = lambda n, sh, dt=F32: es.enter_context(nc.psum_tensor(n, sh, dt))

    wqkv = SB("wqkv_sb", [128, 8 * 3072], BF16)
    woutS = SB("woutS", [128, 8 * 1024], BF16)
    qrhs = SB("qrhs", [128, 8 * 1024], BF16)
    rs_bc = SB("rs_bc", [128, 1024], F32)
    bv_bc = SB("bv_bc", [128, 1024], F32)
    bout_bc = SB("bout_bc", [128, 1024], F32)
    bqkv_cm = SB("bqkv_cmS", [128, 16], F32)
    omega = SB("omegaS", [128, 1024], BF16)
    idnb = SB("idnbS", [128, 128], BF16)
    idnf = SB("idnfS", [128, 128], F32)
    tri = SB("triS", [128, 128], BF16)
    onesc = SB("onescS", [128, 1], BF16)
    ones2 = SB("ones2S", [128, 2], BF16)
    gate = SB("gateS", [128, 1], F32)
    sel = SB("selS", [32, 16 * 128], BF16)
    eyer = SB("eyerS", [1, 1024], F32)
    qnc = SB("qncS", [2, 12], F32)
    selpT = SB("selpTS", [2, 128], F32)
    selp2 = SB("selp2S", [2, 64], F32)
    s2s_row = SB("s2srowS", [1, 64], F32)
    one1 = SB("one1S", [1, 1], F32)
    F32A = SB("f32a", [128, 2048], F32)
    pat = F32A[0:1, 0:128]
    blkd = F32A[0:1, 1024:1536]
    biasq = SB("biasqS", [128, 1], F32)
    lo_u = SB("lo_uS", [128, 1], F32)
    hi_u = SB("hi_uS", [128, 1], F32)
    sw2c = SB("sw2cS", [128, 1], F32)
    Pctx = SB("PctxS", [128, 512], F32)
    Pden = SB("PdenS", [128, 16], F32)
    PctxBe = SB("PctxBeS", [128, 512], BF16)
    PctxBo = SB("PctxBoS", [128, 512], BF16)
    PdenBe = SB("PdenBeS", [128, 16], BF16)
    PdenBo = SB("PdenBoS", [128, 16], BF16)
    onesc2 = SB("onesc2S", [128, 2], BF16)
    zpad = SB("zpadS", [128, 128], BF16)
    rs_tm = SB("rs_tmS", [128, 8], F32)

    xbuf = [SB("xbuf0", [128, 4096]), SB("xbuf1", [128, 4096])]
    xln_tm = SB("xln_tm", [128, 1024], BF16)
    xln_fm = SB("xln_fm", [128, 4096], BF16)
    qfm = SB("qfm", [128, 4096], BF16)
    kfm = SB("kfm", [128, 4096], BF16)
    qsq = SB("qsq", [128, 512], BF16)
    vtm = SB("vtm", [128, 4096], BF16)
    qf = SB("qf", [128, 4096], BF16)
    kf = SB("kf", [128, 4096], BF16)
    recip_tm = SB("recip_tm", [128, 128], F32)
    recip_fm = SB("recip_fm", [32, 512], BF16)
    attn_fm = SB("attn_fm", [128, 1024], BF16)
    den_sb = SB("den_sb", [128, 64], F32)
    ffn_sb = SB("ffn_sb", [128, 1024], F32)
    stat = SB("stat", [128, 64], F32)

    attn_tm = qfm
    kftm = xln_fm
    smb_all = [xln_tm, attn_fm]
    qlhs_tm = lambda: kfm[:, 0:1024]
    qlhs_fm = lambda: kfm[:, 1024:2048]
    recipb_sb = [F32A[:, 0:512], F32A[:, 512:1024]]
    u_sb = [F32A[:, 1024:1536], F32A[:, 1536:2048]]
    scr_s = F32A[:, 0:1024]
    ln3b = F32A[:, 1024:2048]

    PB = [PS("pb%d" % i, [128, 512], F32) for i in range(7)]
    PT = PS("pt", [128, 1024], BF16)

    fz = Fence(nc)
    t_, v_, s_, g_, y_ = nc.tensor, nc.vector, nc.scalar, nc.gpsimd, nc.sync

    fz.twritten = {}

    def _tflush():
        if not fz.twritten:
            return 0
        t_.drain()
        items = sorted(fz.twritten.values(), key=lambda it: it[3])
        n = 0
        cur_lb = None
        for cell, P, ob, lb in items:
            if cur_lb is not None and lb != cur_lb:
                t_.drain()
            cur_lb = lb
            ins = t_.matmul(cell, zpad[lb:lb + 1, 0:P], zpad[lb:lb + 1, 0:2],
                            start=False, stop=True, skip_group_check=True,
                            tile_position=(lb, ob))
            ins.then_inc(fz.sem["T"], 1)
            n += 1
        t_.drain()
        fz.twritten.clear()
        return n
    fz.tflush_cb = _tflush

    def mm(out, lhsT, rhs, start, stop):
        ob = out.offset // max(out.ap[0][0], 1)
        lb = lhsT.offset // max(lhsT.ap[0][0], 1)
        c0 = out.offset % max(out.ap[0][0], 1)
        c0 -= c0 % 2
        P = out.ap[0][1]
        key = (out.tensor.name, int(ob), int(lb))
        fz.twritten[key] = (bass.AP(out.tensor, ob * out.ap[0][0] + c0,
                                    [[out.ap[0][0], int(P)], [1, 2]]),
                            int(P), int(ob), int(lb))
        return fz.put("T", t_.matmul(out, lhsT, rhs, start=start, stop=stop,
                                     skip_group_check=bool(ob),
                                     tile_position=(int(lb), int(ob))))

    def tr(out, in_, idn):
        return fz.put("T", t_.transpose(out, in_, idn))

    for kc in range(8):
        fz.dma("GD", g_.dma_start(out=wqkv[:, kc * 3072:(kc + 1) * 3072],
                                  in_=d_wqkv[kc * 128:(kc + 1) * 128, :]))
        fz.dma("GD", g_.dma_start(out=woutS[:, kc * 1024:(kc + 1) * 1024],
                                  in_=d_wout[kc * 128:(kc + 1) * 128, :]))
    fz.dma("GD", g_.dma_start(out=omega[:, :], in_=d_omega[:, :]))
    for dst, src in [(bqkv_cm, d_bqkv_cm), (bv_bc, d_bv), (bout_bc, d_bout),
                     (gate, d_gate), (idnb, d_idnb),
                     (idnf, d_idnf), (tri, d_tri), (onesc, d_ones)]:
        fz.dma("LD", y_.dma_start(out=dst[:, :], in_=src[:, :]))
    fz.dma("LD", y_.dma_start(out=eyer[:, :], in_=d_eyer[:, :]))
    fz.dma("LD", y_.dma_start(out=selpT[:, :], in_=d_selpT[:, :]))
    fz.dma("LD", y_.dma_start(out=selp2[:, :], in_=d_selp2[:, :]))
    fz.dma("LD", y_.dma_start(out=qnc[0:2, 0:1], in_=d_qn[0:1, 0:2]))
    fz.dma("LD", y_.dma_start(out=qnc[0:2, 1:2], in_=d_qw[0:1, 0:2]))

    fz.put("V", v_.memset(recip_tm[:, :], 1.0))
    fz.put("V", v_.memset(Pctx[:, :], 0.0))
    fz.put("V", v_.memset(Pden[:, :], 0.0))
    fz.put("V", v_.memset(PctxBe[:, :], 0.0))
    fz.put("V", v_.memset(PctxBo[:, :], 0.0))
    fz.put("V", v_.memset(PdenBe[:, :], 0.0))
    fz.put("V", v_.memset(PdenBo[:, :], 0.0))
    fz.put("V", v_.memset(one1[:, :], 1.0))
    fz.put("V", v_.memset(onesc2[:, :], 1.0))
    fz.put("V", v_.memset(zpad[:, :], 0.0))
    fz.put("V", v_.memset(ones2[:, :], 0.0))
    fz.fence(["V"], ["V"])
    fz.put("V", v_.memset(ones2[0:64, 0:1], 1.0))
    fz.put("V", v_.memset(ones2[64:128, 1:2], 1.0))

    fz.fence(["LD"], ["V", "S", "T"])
    fz.put("V", v_.tensor_scalar(qnc[0:2, 2:3], qnc[0:2, 0:1], 0.0, 2.0, ALU.max, ALU.mult))
    fz.put("V", v_.tensor_scalar(qnc[0:2, 8:9], qnc[0:2, 1:2], 0.0, 1.0 / 32.0, ALU.max, ALU.mult))
    fz.put("V", v_.tensor_scalar(qnc[0:2, 5:6], qnc[0:2, 0:1], -1.0, None, ALU.mult))
    fz.put("V", v_.tensor_scalar(qnc[0:2, 6:7], qnc[0:2, 0:1], -20.0, None, ALU.add))
    fz.put("V", v_.tensor_scalar(qnc[0:2, 7:8], qnc[0:2, 0:1], 20.0, None, ALU.add))
    fz.fence(["V"], ["S"])
    fz.put("S", s_.activation(out=qnc[0:2, 3:4], in_=qnc[0:2, 2:3], func=AF.Sqrt))
    fz.put("S", s_.copy(recip_tm[:, 0:64], recip_tm[:, 0:64]))
    fz.fence(["V", "S"], ["T"])
    mm(PB[2][:, 0:2], selpT[0:2, :], qnc[0:2, 5:7], True, True)
    mm(PB[3][:, 0:2], selpT[0:2, :], qnc[0:2, 7:9], True, True)
    mm(PB[1][0:1, 0:64], qnc[0:2, 3:4], selp2[0:2, :], True, True)
    fz.fence(["T"], ["V"])
    fz.put("V", v_.tensor_copy(biasq[:, :], PB[2][:, 0:1]))
    fz.put("V", v_.tensor_copy(lo_u[:, :], PB[2][:, 1:2]))
    fz.put("V", v_.tensor_copy(hi_u[:, :], PB[3][:, 0:1]))
    fz.put("V", v_.tensor_copy(sw2c[:, :], PB[3][:, 1:2]))
    fz.put("V", v_.tensor_copy(s2s_row[:, :], PB[1][0:1, 0:64]))
    fz.put("V", v_.tensor_copy(recip_tm[:, 0:64], recip_tm[:, 0:64]))
    fz.fence(["V"], ["T"])
    for qk in range(2):
        for hp in range(8):
            bi = qk * 8 + hp
            pb = PB[2 + bi // 4]
            for half in range(2):
                row = qk * 16 + 2 * hp + half
                mm(pb[0:32, (bi % 4) * 128 + half * 64: (bi % 4) * 128 + half * 64 + 64],
                   eyer[0:1, row * 32:(row + 1) * 32], s2s_row[0:1, :],
                   True, True)
    fz.fence(["T"], ["V"])
    for g4 in range(4):
        fz.put("V", v_.tensor_copy(sel[:, g4 * 512:(g4 + 1) * 512],
                                   PB[2 + g4][0:32, 0:512]))
    fz.fence(["V"], ["T", "S"])

    for jc in range(8):
        buf = xbuf[jc // 4]
        fz.dma("LD", y_.dma_start(out=buf[:, (jc % 4) * 1024:(jc % 4 + 1) * 1024],
                                  in_=d_rhsT[jc * 128:(jc + 1) * 128, :]))
    fz.fence(["LD"], ["V"])
    for jc in range(8):
        src = xbuf[jc // 4][:, (jc % 4) * 1024:(jc % 4 + 1) * 1024]
        fz.put("V", v_.tensor_reduce(stat[:, 0:1], src, AX.X, ALU.max,
                                     apply_absolute_value=True))
        fz.fence(["V"], ["V"])
        fz.put("V", v_.tensor_scalar(stat[:, 1:2], stat[:, 0:1], 1e-12, None, ALU.max))
        fz.fence(["V"], ["V"])
        fz.put("V", v_.tensor_scalar(rs_tm[:, jc:jc + 1], stat[:, 1:2], 1.0 / 127.0,
                                     None, ALU.mult))
        fz.put("V", v_.reciprocal(stat[:, 2:3], stat[:, 1:2]))
        fz.fence(["V"], ["V"])
        fz.put("V", v_.tensor_scalar(stat[:, 3:4], stat[:, 2:3], 127.0, None, ALU.mult))
        dstq = (qf if jc < 4 else kf)[:, (jc % 4) * 1024:(jc % 4 + 1) * 1024]
        for hf in range(2):
            fz.fence(["V"], ["V"])
            fz.put("V", v_.tensor_scalar(u_sb[0][:, :], src[:, hf * 512:(hf + 1) * 512],
                                         stat[:, 3:4], MAGIC, ALU.mult, ALU.add))
            fz.fence(["V"], ["V"])
            fz.put("V", v_.tensor_scalar(dstq[:, hf * 512:(hf + 1) * 512], u_sb[0][:, :],
                                         -MAGIC, None, ALU.add))
    fz.fence(["V"], ["T"])
    for cc in range(8):
        for jc in range(8):
            src = (qf if jc < 4 else kf)[:, (jc % 4) * 1024 + cc * 128:
                                         (jc % 4) * 1024 + cc * 128 + 128]
            tr(PT[:, jc * 128:(jc + 1) * 128], src, idnb[:, :])
        fz.fence(["T"], ["V"])
        fz.put("V", v_.tensor_copy(qrhs[:, cc * 1024:(cc + 1) * 1024], PT[:, :]))
        fz.fence(["V"], ["T"])
    fz.put("V", v_.memset(pat[:, :], 1.0))
    for hf in range(2):
        fz.fence(["V"], ["T"])
        for q in range(4):
            jc = hf * 4 + q
            tr(PB[1][0:1, q * 128:(q + 1) * 128], rs_tm[:, jc:jc + 1], idnf[:, :])
        fz.fence(["T"], ["V"])
        fz.put("V", v_.tensor_copy(blkd[0:1, 0:512], PB[1][0:1, 0:512]))
        fz.fence(["V"], ["T"])
        mm(PB[2][:, :], pat[0:1, :], blkd[0:1, 0:512], True, True)
        fz.fence(["T"], ["V"])
        fz.put("V", v_.tensor_copy(rs_bc[:, hf * 512:(hf + 1) * 512], PB[2][:, :]))
    fz.fence(["V", "GD"], ["T", "S", "Y"])

    def stats1(src):
        fz.put("V", v_.tensor_reduce(stat[:, 16:17], src, AX.X, ALU.add))
        fz.put("S", s_.activation(out=scr_s[:, :], in_=src, func=AF.Square,
                                  accum_out=stat[:, 24:25]))
        fz.fence(["V", "S"], ["V"])
        fz.put("V", v_.tensor_scalar(stat[:, 32:33], stat[:, 16:17], 1.0 / 1024.0,
                                     None, ALU.mult))
        fz.put("V", v_.scalar_tensor_tensor(stat[:, 40:41], stat[:, 16:17],
                                            1.0 / (1024.0 * 1024.0), stat[:, 16:17],
                                            ALU.mult, ALU.mult))
        fz.put("V", v_.tensor_scalar(stat[:, 44:45], stat[:, 24:25], 1.0 / 1024.0,
                                     1e-5, ALU.mult, ALU.add))
        fz.fence(["V"], ["V"])
        fz.put("V", v_.tensor_tensor(stat[:, 48:49], stat[:, 44:45], stat[:, 40:41],
                                     ALU.subtract))
        fz.fence(["V"], ["V"])
        fz.put("V", v_.reciprocal(stat[:, 52:53], stat[:, 48:49]))
        fz.fence(["V"], ["S"])
        fz.put("S", s_.activation(out=stat[:, 0:1], in_=stat[:, 52:53], func=AF.Sqrt))
        fz.fence(["S"], ["V"])
        fz.put("V", v_.scalar_tensor_tensor(stat[:, 8:9], stat[:, 32:33], -1.0,
                                            stat[:, 0:1], ALU.mult, ALU.mult))
        fz.fence(["V"], ["S"])

    chunk_end_V = []
    store_cnt_after = []

    def emit_chunk(ci, src_dram, src_row0, kpass_only, out_row0=None):
        par = ci % 2
        xb = xbuf[par]

        if ci >= 2:
            fz.flush("V")
            y_.wait_ge(fz.sem["V"], chunk_end_V[ci - 2])
            if store_cnt_after[ci - 2] > 0:
                y_.wait_ge(fz.sem["ST"], store_cnt_after[ci - 2])
        src3 = src_dram.rearrange("(n p) c -> p n c", p=128)
        fz.dma("LD", y_.dma_start(
            out=xb[:, :].rearrange("p (n c) -> p n c", c=1024),
            in_=src3[:, src_row0 // 128: src_row0 // 128 + NT, :]))
        fz.fence(["LD"], ["V", "S"])

        def tri_probe(tag):
            if STOP_AFTER == tag:
                fz.fence(["T", "V", "S"], ["V"])
                fz.put("V", v_.tensor_copy(xb[:, 0:128], tri[:, :]))
                fz.fence(["V"], ["Y", "T", "S"])
                return True
            return False

        if STOP_AFTER == "dump_tri":
            fz.fence(["T", "V", "S"], ["V"])
            fz.put("V", v_.tensor_copy(xb[:, 0:128], tri[:, :]))
            fz.put("V", v_.tensor_copy(xb[:, 128:129], biasq[:, :]))
            fz.put("V", v_.tensor_copy(xb[:, 132:164], den_sb[:, 0:32]))
            fz.fence(["V"], ["Y", "T", "S"])
            return
        for j in range(NT):
            stats1(xb[:, j * 1024:(j + 1) * 1024])
            xl = xln_tm[:, 0:1024]
            fz.put("S", s_.activation(out=xl, in_=xb[:, j * 1024:(j + 1) * 1024],
                                      func=AF.Identity, bias=stat[:, 8:9],
                                      scale=stat[:, 0:1]))
            fz.fence(["S"], ["T"])
            for kc in range(8):
                tr(PT[:, kc * 128:(kc + 1) * 128], xl[:, kc * 128:(kc + 1) * 128],
                   idnb[:, :])
            fz.fence(["T"], ["V"])
            dst = xln_fm[:, :].rearrange("p (k t) -> p k t", t=512)[:, :, j * 128:(j + 1) * 128]
            fz.put("V", v_.tensor_copy(dst, PT[:, :].rearrange("p (k t) -> p k t", t=128)))
            fz.fence(["V"], ["T", "S"])

        if STOP_AFTER == "ln1" or tri_probe("tri_ln1"):
            return
        ccs = list(range(8, 16)) if kpass_only else list(range(16))
        sS = {}

        def norm_mms(i):
            cc = ccs[i]
            hp = cc % 8
            qk = 0 if cc < 8 else 1
            qs = (qsq if i % 2 == 0 else attn_fm)[:, 0:512]
            t_.wait_ge(fz.sem["S"], sS[i])
            for j in range(NT):
                col = j * 32 + qk * 16 + 2 * hp
                mm(PB[6][:, col:col + 2], qs[:, j * 128:(j + 1) * 128], ones2[:, :],
                   True, True)

        for i, cc in enumerate(ccs):
            pb = PB[i % 2]
            if i >= 2:
                t_.wait_ge(fz.sem["S"], sS[i - 2])
            for kc in range(8):
                mm(pb[:, :], wqkv[:, kc * 3072 + cc * 128: kc * 3072 + (cc + 1) * 128],
                   xln_fm[:, kc * 512:(kc + 1) * 512], kc == 0, kc == 7)
            fz.fence(["T"], ["S"])
            hp = cc % 8
            dst = qfm if cc < 8 else kfm
            fz.put("S", s_.activation(out=dst[:, hp * 512:(hp + 1) * 512], in_=pb[:, :],
                                      func=AF.Identity, bias=bqkv_cm[:, cc:cc + 1]))
            fz.put("S", s_.activation(out=(qsq if i % 2 == 0 else attn_fm)[:, 0:512],
                                      in_=pb[:, :], func=AF.Square,
                                      bias=bqkv_cm[:, cc:cc + 1]))
            fz.flush("S")
            sS[i] = fz.cnt["S"]
            if i >= 1:
                norm_mms(i - 1)
        norm_mms(len(ccs) - 1)
        if STOP_AFTER == "qkv" or tri_probe("tri_qkv"):
            return
        for j in range(NT):
            for half in range(2):
                pb = PB[2 + half]
                for kc in range(8):
                    mm(pb[:, :],
                       xln_fm[:, kc * 512 + j * 128: kc * 512 + (j + 1) * 128],
                       wqkv[:, kc * 3072 + 2048 + half * 512:
                            kc * 3072 + 2048 + (half + 1) * 512],
                       kc == 0, kc == 7)
            fz.fence(["T"], ["V"])
            for half in range(2):
                fz.put("V", v_.tensor_tensor(
                    vtm[:, j * 1024 + half * 512: j * 1024 + (half + 1) * 512],
                    PB[2 + half][:, :], bv_bc[:, half * 512:(half + 1) * 512], ALU.add))
            fz.fence(["V"], ["T"])

        if STOP_AFTER == "v":
            return
        fz.fence(["T"], ["V"])
        qks = [1] if kpass_only else [0, 1]
        slots = [(j, qk) for j in range(NT) for qk in qks]
        for i, (j, qk) in enumerate(slots):
            sl = slice(j * 32 + qk * 16, j * 32 + qk * 16 + 16)
            fz.put("V", v_.tensor_scalar(scr_s[:, i * 16:(i + 1) * 16], PB[6][:, sl],
                                         1e-24, None, ALU.max))
        fz.fence(["V"], ["V"])
        for i, (j, qk) in enumerate(slots):
            sl = slice(j * 32 + qk * 16, j * 32 + qk * 16 + 16)
            fz.put("V", v_.reciprocal(recip_tm[:, sl], scr_s[:, i * 16:(i + 1) * 16]))
        fz.fence(["V"], ["S"])
        for j in range(NT):
            for qk in qks:
                sl = slice(j * 32 + qk * 16, j * 32 + qk * 16 + 16)
                fz.put("S", s_.activation(out=recip_tm[:, sl], in_=recip_tm[:, sl],
                                          func=AF.Sqrt))
        fz.fence(["S"], ["T"])
        for j in range(NT):
            tr(PB[5][0:32, j * 128:(j + 1) * 128], recip_tm[:, j * 32:(j + 1) * 32],
               idnf[:, :])
        fz.fence(["T"], ["V"])
        fz.put("V", v_.tensor_copy(recip_fm[:, :], PB[5][0:32, :]))
        fz.fence(["V"], ["T"])

        if STOP_AFTER == "norms" or tri_probe("tri_norms"):
            return
        sides = [("k", kfm, kf)] if kpass_only else [("q", qfm, qf), ("k", kfm, kf)]
        for sname, src_fm, dst_f in sides:
            qk = 0 if sname == "q" else 1
            for hp in range(8):
                p2 = hp % 2
                for half in range(2):
                    h = 2 * hp + half
                    b = half * 64
                    mm(PB[4][b:b + 64, :], omega[b:b + 64, h * 64:(h + 1) * 64],
                       src_fm[b:b + 64, hp * 512:(hp + 1) * 512], True, True)
                mm(PB[5][:, :], sel[:, (qk * 8 + hp) * 128:(qk * 8 + hp + 1) * 128],
                   recip_fm[:, :], True, True)
                fz.fence(["T"], ["S"])
                fz.put("S", s_.copy(recipb_sb[p2][:, :], PB[5][:, :]))
                fz.fence(["S"], ["V"])
                fz.put("V", v_.tensor_tensor(u_sb[p2][:, :], PB[4][:, :],
                                             recipb_sb[p2][:, :], ALU.mult))
                fz.fence(["V"], ["V"])
                fz.put("V", v_.tensor_scalar(u_sb[p2][:, :], u_sb[p2][:, :],
                                             lo_u[:, :], hi_u[:, :], ALU.max, ALU.min))
                fz.fence(["V"], ["S", "T"])
                fz.put("S", s_.activation(out=dst_f[:, hp * 512:(hp + 1) * 512],
                                          in_=u_sb[p2][:, :], func=AF.Exp,
                                          bias=biasq[:, :]))
            fz.fence(["S"], ["V"])
            if sname == "k":
                for hp in range(8):
                    fz.put("V", v_.tensor_scalar(dst_f[:, hp * 512:(hp + 1) * 512],
                                                 dst_f[:, hp * 512:(hp + 1) * 512],
                                                 sw2c[:, :], None, ALU.mult))
        fz.fence(["V", "S"], ["T"])

        if STOP_AFTER == "features" or tri_probe("tri_feat"):
            return
        for grp in range(4):
            for pp in range(2):
                hp = grp * 2 + pp
                for j in range(NT):
                    tr(PT[:, pp * 512 + j * 128: pp * 512 + (j + 1) * 128],
                       kf[:, hp * 512 + j * 128: hp * 512 + (j + 1) * 128],
                       idnb[:, :])
            fz.fence(["T"], ["V"])
            fz.put("V", v_.tensor_copy(kftm[:, grp * 1024:(grp + 1) * 1024], PT[:, :]))
            fz.fence(["V"], ["T"])

        if STOP_AFTER == "kftm" or tri_probe("tri_kftm"):
            return
        if STOP_AFTER == "dump_v" and not kpass_only:
            fz.fence(["T", "S"], ["V"])
            fz.put("V", v_.tensor_copy(xb[:, 0:4096], vtm[:, 0:4096]))
            fz.fence(["V"], ["Y", "T", "S"])
            return
        if STOP_AFTER == "dump_P" and not kpass_only:
            fz.fence(["T", "S"], ["V"])
            fz.put("V", v_.tensor_copy(xb[:, 0:512], Pctx[:, 0:512]))
            fz.put("V", v_.tensor_copy(xb[:, 512:520], Pden[:, 0:8]))
            fz.put("V", v_.tensor_copy(xb[:, 1024:2048], kftm[:, 0:1024]))
            fz.fence(["V"], ["Y", "T", "S"])
            return
        if (not kpass_only) and STOP_AFTER in ("scan_kv", "scan_kv_even", "pstate"):
            return
        if kpass_only:
            heads = list(range(16))
            if STOP_AFTER == "scan_kv_even":
                heads = [h for h in heads if h % 2 == 0]
            for h in heads:
                b = (h % 2) * 64
                for j in range(NT):
                    st0 = (h in (0, 1) and j == 0) or (STOP_AFTER == "scan_kv_even" and h == 0 and j == 0)
                    sp0 = (h in (14, 15) and j == NT - 1) or (STOP_AFTER == "scan_kv_even" and h == 14 and j == NT - 1)
                    kslc = kftm[:, (h // 2) * 512 + j * 128 + (h % 2) * 64:
                                (h // 2) * 512 + j * 128 + (h % 2) * 64 + 64]
                    mm(PB[4][b:b + 64, 64 * (h // 2): 64 * (h // 2) + 64],
                       kslc,
                       vtm[:, j * 1024 + h * 64: j * 1024 + (h + 1) * 64], st0, sp0)
                    mm(PB[5][b:b + 64, 2 * (h // 2): 2 * (h // 2) + 2],
                       kslc, onesc2[:, :], st0, sp0)
            fz.fence(["T"], ["V"])
            if STOP_AFTER in ("scan_kv", "scan_kv_even"):
                return
            fz.put("V", v_.tensor_tensor(Pctx[:, :], Pctx[:, :], PB[4][:, :], ALU.add))
            fz.put("V", v_.tensor_tensor(Pden[:, :], Pden[:, :], PB[5][:, 0:16], ALU.add))
            fz.fence(["V"], ["T"])
        else:
            for j in range(NT):
                if STOP_AFTER in ("scan1", "dump_den") and j > 0:
                    break
                fz.fence(["T", "V"], ["V"])
                fz.put("V", v_.tensor_copy(PctxBe[0:64, :], Pctx[0:64, :]))
                fz.put("V", v_.tensor_copy(PctxBo[64:128, :], Pctx[64:128, :]))
                fz.put("V", v_.tensor_copy(PdenBe[0:64, :], Pden[0:64, :]))
                fz.put("V", v_.tensor_copy(PdenBo[64:128, :], Pden[64:128, :]))
                fz.fence(["V"], ["T"])
                sc_bank = [PB[2], PB[3], PB[4], PB[6]]
                for half in range(2):
                    for h in range(half, 16, 2):
                        hp = h // 2
                        b = half * 64
                        qfs = qf[b:b + 64, hp * 512 + j * 128: hp * 512 + (j + 1) * 128]
                        kfs = kf[b:b + 64, hp * 512 + j * 128: hp * 512 + (j + 1) * 128]
                        mm(sc_bank[h // 4][:, (h % 4) * 128:(h % 4 + 1) * 128],
                           kfs, qfs, True, True)
                    if half == 0:
                        fz.flush("T")
                fz.fence(["T"], ["V"])
                if tri_probe("tri_pA"):
                    return
                for h in range(16):
                    fz.put("V", v_.tensor_tensor(
                        smb_all[h // 8][:, (h % 8) * 128:(h % 8 + 1) * 128],
                        sc_bank[h // 4][:, (h % 4) * 128:(h % 4 + 1) * 128],
                        tri[:, :], ALU.mult))
                fz.fence(["V"], ["T"])
                if tri_probe("tri_pB"):
                    return
                for h in range(16):
                    hp, half = h // 2, h % 2
                    cb = PB[2 + h // 8]
                    pce = PctxBe if half == 0 else PctxBo
                    pde = PdenBe if half == 0 else PdenBo
                    qfs_full = qf[:, hp * 512 + j * 128: hp * 512 + (j + 1) * 128]
                    smbs = smb_all[h // 8][:, (h % 8) * 128:(h % 8 + 1) * 128]
                    mm(cb[:, (h % 8) * 64:(h % 8) * 64 + 64], qfs_full,
                       pce[:, 64 * (h // 2): 64 * (h // 2) + 64], True, False)
                    mm(cb[:, (h % 8) * 64:(h % 8) * 64 + 64], smbs,
                       vtm[:, j * 1024 + h * 64: j * 1024 + (h + 1) * 64],
                       False, True)
                    mm(PB[5][:, 16 + 2 * h: 18 + 2 * h], qfs_full,
                       pde[:, 2 * (h // 2): 2 * (h // 2) + 2], True, False)
                    mm(PB[5][:, 16 + 2 * h: 18 + 2 * h], smbs,
                       onesc2[:, :], False, True)
                for h in range(16):
                    b = (h % 2) * 64
                    kslc = kftm[:, (h // 2) * 512 + j * 128 + (h % 2) * 64:
                                (h // 2) * 512 + j * 128 + (h % 2) * 64 + 64]
                    mm(PB[4][b:b + 64, 64 * (h // 2): 64 * (h // 2) + 64],
                       kslc,
                       vtm[:, j * 1024 + h * 64: j * 1024 + (h + 1) * 64],
                       True, True)
                    mm(PB[5][b:b + 64, 2 * (h // 2): 2 * (h // 2) + 2],
                       kslc, onesc2[:, :], True, True)
                fz.fence(["T"], ["V"])
                if tri_probe("tri_pC"):
                    return
                fz.put("V", v_.tensor_scalar(den_sb[:, 32:64], PB[5][:, 16:48], 1e-6,
                                             None, ALU.max))
                fz.fence(["V"], ["V"])
                fz.put("V", v_.reciprocal(den_sb[:, 0:32], den_sb[:, 32:64]))
                fz.fence(["V"], ["V"])
                if STOP_AFTER == "dump_den" and j == 0:
                    fz.put("V", v_.tensor_copy(xb[:, 0:32], den_sb[:, 0:32]))
                    fz.put("V", v_.tensor_copy(xb[:, 64:576], PB[2][:, :]))
                    fz.put("V", v_.tensor_copy(xb[:, 1024:2048], smb_all[0][:, :]))
                    fz.put("V", v_.tensor_copy(xb[:, 2048:2064], PB[5][:, 0:16]))
                    fz.put("V", v_.tensor_copy(xb[:, 2176:2304], tri[:, :]))
                    fz.put("V", v_.tensor_copy(xb[:, 2304:2432], sc_bank[0][:, 0:128]))
                    fz.fence(["V"], ["Y", "T", "S"])
                for h in range(16):
                    fz.put("V", v_.tensor_scalar(
                        attn_tm[:, j * 1024 + h * 64: j * 1024 + (h + 1) * 64],
                        PB[2 + h // 8][:, (h % 8) * 64:(h % 8) * 64 + 64],
                        den_sb[:, 2 * h:2 * h + 1], None, ALU.mult))
                fz.put("V", v_.tensor_tensor(Pctx[:, :], Pctx[:, :], PB[4][:, :],
                                             ALU.add))
                fz.put("V", v_.tensor_tensor(Pden[:, :], Pden[:, :], PB[5][:, 0:16],
                                             ALU.add))
                fz.fence(["V"], ["T"])

            for j in range(NT):
                if STOP_AFTER == "dump_cols":
                    fz.put("V", v_.tensor_copy(xb[:, j * 1024 + 0: j * 1024 + 1], biasq[:, :]))
                    fz.put("V", v_.tensor_copy(xb[:, j * 1024 + 1: j * 1024 + 2], lo_u[:, :]))
                    fz.put("V", v_.tensor_copy(xb[:, j * 1024 + 2: j * 1024 + 3], hi_u[:, :]))
                    fz.put("V", v_.tensor_copy(xb[:, j * 1024 + 3: j * 1024 + 4], sw2c[:, :]))
                    fz.put("V", v_.tensor_copy(xb[0:2, j * 1024 + 4: j * 1024 + 8], qnc[0:2, 0:4]))
                    fz.put("V", v_.tensor_copy(xb[0:1, j * 1024 + 8: j * 1024 + 520], recipb_sb[0][0:1, :]))
                    fz.fence(["V"], ["Y", "T", "S"])
                    continue
                if STOP_AFTER == "dump_attn":
                    fz.put("V", v_.tensor_copy(xb[:, j * 1024:(j + 1) * 1024],
                                               attn_tm[:, j * 1024:(j + 1) * 1024]))
                    fz.fence(["V"], ["Y", "T", "S"])
                    continue
                if STOP_AFTER == "dump_qf":
                    fz.put("V", v_.tensor_copy(xb[:, j * 1024:(j + 1) * 1024],
                                               qf[:, j * 1024:(j + 1) * 1024]))
                    fz.fence(["V"], ["Y", "T", "S"])
                    continue
                if STOP_AFTER == "dump_kf":
                    fz.put("V", v_.tensor_copy(xb[:, j * 1024:(j + 1) * 1024],
                                               kf[:, j * 1024:(j + 1) * 1024]))
                    fz.fence(["V"], ["Y", "T", "S"])
                    continue
                for cc in range(8):
                    tr(PT[:, cc * 128:(cc + 1) * 128],
                       attn_tm[:, j * 1024 + cc * 128: j * 1024 + cc * 128 + 128],
                       idnb[:, :])
                fz.fence(["T"], ["V"])
                fz.put("V", v_.tensor_copy(attn_fm[:, :], PT[:, :]))
                fz.fence(["V"], ["T"])
                for half in range(2):
                    pb = PB[half]
                    for kc in range(8):
                        mm(pb[:, :], attn_fm[:, kc * 128:(kc + 1) * 128],
                           woutS[:, kc * 1024 + half * 512:
                                 kc * 1024 + (half + 1) * 512], kc == 0, kc == 7)
                fz.fence(["T"], ["V"])
                for half in range(2):
                    fz.put("V", v_.tensor_tensor(
                        ffn_sb[:, half * 512:(half + 1) * 512], PB[half][:, :],
                        bout_bc[:, half * 512:(half + 1) * 512], ALU.add))
                if STOP_AFTER == "dump_aproj":
                    fz.put("V", v_.tensor_copy(xb[:, j * 1024:(j + 1) * 1024], ffn_sb[:, :]))
                    fz.fence(["V"], ["Y", "T", "S"])
                    continue
                fz.fence(["V"], ["V", "S"])
                stats1(ffn_sb[:, :])
                fz.put("S", s_.activation(out=scr_s[:, :], in_=ffn_sb[:, :],
                                          func=AF.Identity, bias=stat[:, 8:9],
                                          scale=stat[:, 0:1]))
                fz.fence(["S"], ["V"])
                fz.put("V", v_.tensor_tensor(xb[:, j * 1024:(j + 1) * 1024],
                                             scr_s[:, :],
                                             xb[:, j * 1024:(j + 1) * 1024], ALU.add))
                if STOP_AFTER == "dump_x2":
                    fz.fence(["V"], ["Y", "T", "S"])
                    continue
                if STOP_AFTER == "ep_out":
                    continue
                fz.fence(["V"], ["V", "S"])
                stats1(xb[:, j * 1024:(j + 1) * 1024])
                fz.put("S", s_.activation(out=ln3b[:, :],
                                          in_=xb[:, j * 1024:(j + 1) * 1024],
                                          func=AF.Identity, bias=stat[:, 8:9],
                                          scale=stat[:, 0:1]))
                fz.fence(["S"], ["V"])
                fz.put("V", v_.tensor_reduce(stat[:, 16:17], ln3b[:, :], AX.X, ALU.max,
                                             apply_absolute_value=True))
                fz.fence(["V"], ["V"])
                fz.put("V", v_.tensor_scalar(stat[:, 17:18], stat[:, 16:17], 1e-12,
                                             None, ALU.max))
                fz.fence(["V"], ["V"])
                fz.put("V", v_.tensor_scalar(stat[:, 18:19], stat[:, 17:18],
                                             1.0 / 127.0, None, ALU.mult))
                fz.put("V", v_.reciprocal(stat[:, 19:20], stat[:, 17:18]))
                fz.fence(["V"], ["V"])
                fz.put("V", v_.tensor_scalar(stat[:, 20:21], stat[:, 19:20], 127.0,
                                             None, ALU.mult))
                fz.fence(["V"], ["V"])
                fz.put("V", v_.tensor_scalar(scr_s[:, :], ln3b[:, :], stat[:, 20:21],
                                             MAGIC, ALU.mult, ALU.add))
                fz.fence(["V"], ["V"])
                fz.put("V", v_.tensor_scalar(qlhs_tm()[:, :], scr_s[:, :], -MAGIC,
                                             None, ALU.add))
                fz.fence(["V"], ["T"])
                for cc in range(8):
                    tr(PT[:, cc * 128:(cc + 1) * 128],
                       qlhs_tm()[:, cc * 128:(cc + 1) * 128], idnb[:, :])
                fz.fence(["T"], ["V"])
                fz.put("V", v_.tensor_copy(qlhs_fm()[:, :], PT[:, :]))
                fz.fence(["V"], ["T"])
                for half in range(2):
                    pb = PB[half]
                    for kc in range(8):
                        mm(pb[:, :], qlhs_fm()[:, kc * 128:(kc + 1) * 128],
                           qrhs[:, kc * 1024 + half * 512:
                                kc * 1024 + (half + 1) * 512], kc == 0, kc == 7)
                fz.fence(["T"], ["V"])
                for half in range(2):
                    fz.put("V", v_.scalar_tensor_tensor(
                        ffn_sb[:, half * 512:(half + 1) * 512], PB[half][:, :],
                        stat[:, 18:19], rs_bc[:, half * 512:(half + 1) * 512],
                        ALU.mult, ALU.mult))
                if STOP_AFTER == "ep_ffn":
                    continue
                if STOP_AFTER == "dump_ffn":
                    fz.put("V", v_.tensor_copy(xb[:, j * 1024:(j + 1) * 1024], ffn_sb[:, :]))
                    fz.fence(["V"], ["Y", "T", "S"])
                    continue
                if STOP_AFTER == "dump_rs":
                    fz.put("V", v_.tensor_copy(xb[:, j * 1024:(j + 1) * 1024], rs_bc[:, :]))
                    fz.fence(["V"], ["Y", "T", "S"])
                    continue
                if STOP_AFTER == "dump_ln3":
                    fz.put("V", v_.tensor_copy(xb[:, j * 1024:(j + 1) * 1024], ln3b[:, :]))
                    fz.fence(["V"], ["Y", "T", "S"])
                    continue
                if STOP_AFTER == "dump_qlhstm":
                    fz.put("V", v_.tensor_copy(xb[:, j * 1024:(j + 1) * 1024], qlhs_tm()[:, :]))
                    fz.fence(["V"], ["Y", "T", "S"])
                    continue
                if STOP_AFTER == "dump_stat":
                    fz.put("V", v_.tensor_copy(xb[:, j * 1024: j * 1024 + 64], stat[:, :]))
                    fz.fence(["V"], ["Y", "T", "S"])
                    continue
                if STOP_AFTER == "dump_qlhs":
                    fz.put("V", v_.tensor_copy(xb[:, j * 1024:(j + 1) * 1024], qlhs_fm()[:, :]))
                    fz.fence(["V"], ["Y", "T", "S"])
                    continue
                fz.fence(["V"], ["V", "S"])
                stats1(ffn_sb[:, :])
                fz.put("S", s_.activation(out=scr_s[:, :], in_=ffn_sb[:, :],
                                          func=AF.Identity, bias=stat[:, 8:9],
                                          scale=stat[:, 0:1]))
                fz.fence(["S"], ["V"])
                fz.put("V", v_.tensor_tensor(xb[:, j * 1024:(j + 1) * 1024],
                                             scr_s[:, :],
                                             xb[:, j * 1024:(j + 1) * 1024], ALU.add))
                fz.fence(["V"], ["Y", "T", "S"])
            dst3 = d_y.rearrange("(n p) c -> p n c", p=128)
            fz.dma("ST", y_.dma_start(
                out=dst3[:, out_row0 // 128: out_row0 // 128 + NT, :],
                in_=xb[:, :].rearrange("p (n c) -> p n c", c=1024)))

        fz.flush("V")
        chunk_end_V.append(fz.cnt["V"])
        store_cnt_after.append(fz.cnt["ST"])

    if STOP_AFTER != "setup":
        for ci in range(NCH_PRE):
            emit_chunk(ci, d_xpre, ci * CH, True)
        fz.fence(["T", "V", "S"], ["V"])
        fz.put("V", v_.tensor_scalar(Pctx[:, :], Pctx[:, :], gate[:, :], None, ALU.mult))
        fz.put("V", v_.tensor_scalar(Pden[:, :], Pden[:, :], gate[:, :], None, ALU.mult))
        fz.fence(["V"], ["T", "S"])
        for ci in range(NCH_OWN):
            emit_chunk(NCH_PRE + ci, d_xown, ci * CH, False, out_row0=ci * CH)
    fz.fence(["T", "V", "S"], ["Y"])
    if fz.cnt["ST"] == 0:
        lastbuf = xbuf[(NCH_PRE + NCH_OWN - 1) % 2]
        dst3 = d_y.rearrange("(n p) c -> p n c", p=128)
        fz.dma("ST", y_.dma_start(out=dst3[:, 0:NT, :],
                                  in_=lastbuf[:, :].rearrange("p (n c) -> p n c", c=1024)))
    fz.fence(["T", "V", "S"], ["G"])
    g_.wait_ge(fz.sem["ST"], fz.cnt["ST"])
    g_.wait_ge(fz.sem["LD"], fz.cnt["LD"])

    es.close()
    fz.es.close()
    return nc


_NC_CACHE = {}


def _get_program():
    if "nc" not in _NC_CACHE:
        _NC_CACHE["nc"] = build_program()
    return _NC_CACHE["nc"]


def kernel(L0_attn_x, L0_attn_Wqkv, L0_attn_bqkv, L0_attn_Wout, L0_attn_bout,
           L0_attn_omega, L0_attn_quad_nodes, L0_attn_quad_weights, L0_ffn_rhs):
    x = np.ascontiguousarray(np.asarray(L0_attn_x, dtype=np.float32))
    wqkv = np.ascontiguousarray(np.asarray(L0_attn_Wqkv, dtype=np.float32))
    bqkv = np.asarray(L0_attn_bqkv, dtype=np.float32)
    wout = np.ascontiguousarray(np.asarray(L0_attn_Wout, dtype=np.float32))
    bout = np.asarray(L0_attn_bout, dtype=np.float32)
    omega = np.asarray(L0_attn_omega, dtype=np.float32)
    qn = np.asarray(L0_attn_quad_nodes, dtype=np.float32)
    qw = np.asarray(L0_attn_quad_weights, dtype=np.float32)
    rhs = np.ascontiguousarray(np.asarray(L0_ffn_rhs, dtype=np.float32))

    rhsT = np.ascontiguousarray(rhs.T)
    bqkv_cm = np.ascontiguousarray(bqkv[:2048].reshape(16, 128).T)
    bv_bc = np.ascontiguousarray(np.broadcast_to(bqkv[2048:], (128, 1024)))
    bout_bc = np.ascontiguousarray(np.broadcast_to(bout, (128, 1024)))
    om = omega.transpose(2, 1, 0, 3).reshape(64, 1024)
    omega_h = np.ascontiguousarray(np.concatenate([om, om], axis=0))
    idn_bf = np.eye(128, dtype=np.float32).astype(ml_dtypes.bfloat16)
    idn_dup = np.ascontiguousarray(
        np.concatenate([np.eye(64), np.eye(64)], axis=0)).astype(ml_dtypes.bfloat16)
    idn_f32 = np.eye(128, dtype=np.float32)
    trimask = np.ascontiguousarray(np.triu(np.ones((128, 128)))).astype(ml_dtypes.bfloat16)
    ones_col = np.ones((128, 1), dtype=np.float32).astype(ml_dtypes.bfloat16)
    eye_rows = np.ascontiguousarray(np.eye(32, dtype=np.float32).reshape(1, 1024))
    selpatT = np.zeros((2, 128), dtype=np.float32)
    for jj in range(128):
        selpatT[(jj // 32) % 2, jj] = 1.0
    selpat2 = np.zeros((2, 64), dtype=np.float32)
    for jj in range(64):
        selpat2[jj // 32, jj] = 1.0

    shared = dict(wqkv=wqkv, wout=wout, rhsT=rhsT, bqkv_cm=bqkv_cm,
                  bv_bcast=bv_bc, bout_bcast=bout_bc, omega_h=omega_h,
                  quad_n=qn.reshape(1, 2), quad_w=qw.reshape(1, 2),
                  idn_bf=idn_bf, idn_dup=idn_dup, idn_f32=idn_f32,
                  trimask=trimask, ones_col=ones_col, eye_rows=eye_rows,
                  selpatT=selpatT, selpat2=selpat2)

    in_maps = []
    for c in range(NC_):
        b, half = c // 2, c % 2
        xb = x[b]
        m = dict(shared)
        m["xpre"] = np.ascontiguousarray(xb[0:TPRE])
        m["xown"] = np.ascontiguousarray(xb[half * TOWN:(half + 1) * TOWN])
        m["gate"] = np.full((128, 1), float(half), dtype=np.float32)
        in_maps.append(m)

    nc = _get_program()
    res = run_bass_kernel_spmd(nc, in_maps, core_ids=list(range(NC_)))

    out = np.empty((B, T, C), dtype=np.float32)
    for c in range(NC_):
        b, half = c // 2, c % 2
        out[b, half * TOWN:(half + 1) * TOWN] = res.results[c]["y"]
    return out


# version 10 (final)
# speedup vs baseline: 1.0064x; 1.0064x over previous
import contextlib
import numpy as np
import ml_dtypes

import concourse.bass as bass
import concourse.mybir as mybir
from concourse.bass_utils import run_bass_kernel_spmd

F32 = mybir.dt.float32
BF16 = mybir.dt.bfloat16
ALU = mybir.AluOpType
AF = mybir.ActivationFunctionType
AX = mybir.AxisListType

B, T, C = 4, 8192, 1024
H = 16
NC_ = 8
TOWN = 4096
TPRE = 4096
CH = 512
NT = CH // 128
NCH_OWN = TOWN // CH
NCH_PRE = TPRE // CH
MAGIC = 12582912.0
STOP_AFTER = None


class Fence:

    def __init__(self, nc):
        self.nc = nc
        self.es = contextlib.ExitStack()
        self.sem = {}
        self.cnt = {}
        self.last = {}
        for name in ("T", "V", "S", "LD", "ST", "GD"):
            self.sem[name] = self.es.enter_context(nc.semaphore("sem" + name))
            self.cnt[name] = 0
            self.last[name] = None
        self.eng = {"T": nc.tensor, "V": nc.vector, "S": nc.scalar,
                    "G": nc.gpsimd, "Y": nc.sync}

    def put(self, name, instr):
        self.last[name] = instr
        return instr

    tflush_cb = None
    twritten = None

    def dma(self, name, instr):
        instr.then_inc(self.sem[name], 16)
        self.cnt[name] += 16
        return instr

    def flush(self, name):
        if self.last[name] is not None:
            if name == "T" and self.tflush_cb is not None:
                n = self.tflush_cb()
                if n:
                    self.cnt[name] += n
                    self.last[name] = None
                    return
            self.last[name].then_inc(self.sem[name], 1)
            self.cnt[name] += 1
            self.last[name] = None

    def fence(self, prods, cons):
        for p in prods:
            self.flush(p)
        for c in cons:
            for p in prods:
                if self.cnt[p] > 0:
                    self.eng[c].wait_ge(self.sem[p], self.cnt[p])


def build_program():
    nc = bass.Bass("TRN2", target_bir_lowering=False, debug=False,
                   detect_race_conditions=False)

    d_xpre = nc.dram_tensor("xpre", [TPRE, C], F32, kind="ExternalInput").ap()
    d_xown = nc.dram_tensor("xown", [TOWN, C], F32, kind="ExternalInput").ap()
    d_wqkv = nc.dram_tensor("wqkv", [C, 3 * C], F32, kind="ExternalInput").ap()
    d_wout = nc.dram_tensor("wout", [C, C], F32, kind="ExternalInput").ap()
    d_rhsT = nc.dram_tensor("rhsT", [C, C], F32, kind="ExternalInput").ap()
    d_bqkv_cm = nc.dram_tensor("bqkv_cm", [128, 16], F32, kind="ExternalInput").ap()
    d_bv = nc.dram_tensor("bv_bcast", [128, C], F32, kind="ExternalInput").ap()
    d_bout = nc.dram_tensor("bout_bcast", [128, C], F32, kind="ExternalInput").ap()
    d_omega = nc.dram_tensor("omega_h", [128, 1024], F32, kind="ExternalInput").ap()
    d_qn = nc.dram_tensor("quad_n", [1, 2], F32, kind="ExternalInput").ap()
    d_qw = nc.dram_tensor("quad_w", [1, 2], F32, kind="ExternalInput").ap()
    d_gate = nc.dram_tensor("gate", [128, 1], F32, kind="ExternalInput").ap()
    d_idnb = nc.dram_tensor("idn_bf", [128, 128], BF16, kind="ExternalInput").ap()
    d_idnd = nc.dram_tensor("idn_dup", [128, 64], BF16, kind="ExternalInput").ap()
    d_idnf = nc.dram_tensor("idn_f32", [128, 128], F32, kind="ExternalInput").ap()
    d_tri = nc.dram_tensor("trimask", [128, 128], BF16, kind="ExternalInput").ap()
    d_ones = nc.dram_tensor("ones_col", [128, 1], BF16, kind="ExternalInput").ap()
    d_eyer = nc.dram_tensor("eye_rows", [1, 1024], F32, kind="ExternalInput").ap()
    d_selpT = nc.dram_tensor("selpatT", [2, 128], F32, kind="ExternalInput").ap()
    d_selp2 = nc.dram_tensor("selpat2", [2, 64], F32, kind="ExternalInput").ap()
    d_y = nc.dram_tensor("y", [TOWN, C], F32, kind="ExternalOutput").ap()

    es = contextlib.ExitStack()
    SB = lambda n, sh, dt=F32: es.enter_context(nc.sbuf_tensor(n, sh, dt))
    PS = lambda n, sh, dt=F32: es.enter_context(nc.psum_tensor(n, sh, dt))

    wqkv = SB("wqkv_sb", [128, 8 * 3072], BF16)
    woutS = SB("woutS", [128, 8 * 1024], BF16)
    qrhs = SB("qrhs", [128, 8 * 1024], BF16)
    rs_bc = SB("rs_bc", [128, 1024], F32)
    bv_bc = SB("bv_bc", [128, 1024], F32)
    bout_bc = SB("bout_bc", [128, 1024], F32)
    bqkv_cm = SB("bqkv_cmS", [128, 16], F32)
    omega = SB("omegaS", [128, 1024], BF16)
    idnb = SB("idnbS", [128, 128], BF16)
    idnf = SB("idnfS", [128, 128], F32)
    tri = SB("triS", [128, 128], BF16)
    tri_rep = SB("triRS", [128, 512], BF16)
    onesc = SB("onescS", [128, 1], BF16)
    ones2 = SB("ones2S", [128, 2], BF16)
    gate = SB("gateS", [128, 1], F32)
    sel = SB("selS", [32, 16 * 128], BF16)
    eyer = SB("eyerS", [1, 1024], F32)
    qnc = SB("qncS", [2, 12], F32)
    selpT = SB("selpTS", [2, 128], F32)
    selp2 = SB("selp2S", [2, 64], F32)
    s2s_row = SB("s2srowS", [1, 64], F32)
    one1 = SB("one1S", [1, 1], F32)
    F32A = SB("f32a", [128, 2048], F32)
    pat = F32A[0:1, 0:128]
    blkd = F32A[0:1, 1024:1536]
    biasq = SB("biasqS", [128, 1], F32)
    lo_u = SB("lo_uS", [128, 1], F32)
    hi_u = SB("hi_uS", [128, 1], F32)
    sw2c = SB("sw2cS", [128, 1], F32)
    Pctx = SB("PctxS", [128, 512], F32)
    Pden = SB("PdenS", [128, 16], F32)
    PctxBe = SB("PctxBeS", [128, 512], BF16)
    PctxBo = SB("PctxBoS", [128, 512], BF16)
    PdenBe = SB("PdenBeS", [128, 16], BF16)
    PdenBo = SB("PdenBoS", [128, 16], BF16)
    onesc2 = SB("onesc2S", [128, 2], BF16)
    zpad = SB("zpadS", [128, 128], BF16)
    rs_tm = SB("rs_tmS", [128, 8], F32)

    xbuf = [SB("xbuf0", [128, 4096]), SB("xbuf1", [128, 4096])]
    xln_tm = SB("xln_tm", [128, 1024], BF16)
    xln_fm = SB("xln_fm", [128, 4096], BF16)
    qfm = SB("qfm", [128, 4096], BF16)
    kfm = SB("kfm", [128, 4096], BF16)
    vtm = SB("vtm", [128, 4096], BF16)
    qf = SB("qf", [128, 4096], BF16)
    kf = SB("kf", [128, 4096], BF16)
    recip_tm = SB("recip_tm", [128, 128], F32)
    recip_fm = SB("recip_fm", [32, 512], BF16)
    attn_fm = SB("attn_fm", [128, 1024], BF16)
    den_sb = SB("den_sb", [128, 64], F32)
    ffn_sb = SB("ffn_sb", [128, 1024], F32)
    stat = SB("stat", [128, 64], F32)

    attn_tm = qfm
    kftm = xln_fm
    smb_all = [xln_tm, attn_fm]
    qlhs_tm = lambda: kfm[:, 0:1024]
    qlhs_fm = lambda: kfm[:, 1024:2048]
    recipb_sb = [F32A[:, 0:512], F32A[:, 512:1024]]
    u_sb = [F32A[:, 1024:1536], F32A[:, 1536:2048]]
    scr_s = F32A[:, 0:1024]
    ln3b = F32A[:, 1024:2048]

    PB = [PS("pb%d" % i, [128, 512], F32) for i in range(7)]
    PT = PS("pt", [128, 1024], BF16)

    fz = Fence(nc)
    t_, v_, s_, g_, y_ = nc.tensor, nc.vector, nc.scalar, nc.gpsimd, nc.sync

    fz.twritten = {}

    def _tflush():
        if not fz.twritten:
            return 0
        t_.drain()
        items = sorted(fz.twritten.values(), key=lambda it: it[3])
        n = 0
        cur_lb = None
        for cell, P, ob, lb in items:
            if cur_lb is not None and lb != cur_lb:
                t_.drain()
            cur_lb = lb
            ins = t_.matmul(cell, zpad[lb:lb + 1, 0:P], zpad[lb:lb + 1, 0:2],
                            start=False, stop=True, skip_group_check=True,
                            tile_position=(lb, ob))
            ins.then_inc(fz.sem["T"], 1)
            n += 1
        t_.drain()
        fz.twritten.clear()
        return n
    fz.tflush_cb = _tflush

    def mm(out, lhsT, rhs, start, stop):
        ob = out.offset // max(out.ap[0][0], 1)
        lb = lhsT.offset // max(lhsT.ap[0][0], 1)
        c0 = out.offset % max(out.ap[0][0], 1)
        c0 -= c0 % 2
        P = out.ap[0][1]
        key = (out.tensor.name, int(ob), int(lb))
        fz.twritten[key] = (bass.AP(out.tensor, ob * out.ap[0][0] + c0,
                                    [[out.ap[0][0], int(P)], [1, 2]]),
                            int(P), int(ob), int(lb))
        return fz.put("T", t_.matmul(out, lhsT, rhs, start=start, stop=stop,
                                     skip_group_check=bool(ob),
                                     tile_position=(int(lb), int(ob))))

    def tr(out, in_, idn):
        return fz.put("T", t_.transpose(out, in_, idn))

    for kc in range(8):
        fz.dma("GD", g_.dma_start(out=wqkv[:, kc * 3072:(kc + 1) * 3072],
                                  in_=d_wqkv[kc * 128:(kc + 1) * 128, :]))
        fz.dma("GD", g_.dma_start(out=woutS[:, kc * 1024:(kc + 1) * 1024],
                                  in_=d_wout[kc * 128:(kc + 1) * 128, :]))
    fz.dma("GD", g_.dma_start(out=omega[:, :], in_=d_omega[:, :]))
    for dst, src in [(bqkv_cm, d_bqkv_cm), (bv_bc, d_bv), (bout_bc, d_bout),
                     (gate, d_gate), (idnb, d_idnb),
                     (idnf, d_idnf), (tri, d_tri), (onesc, d_ones)]:
        fz.dma("LD", y_.dma_start(out=dst[:, :], in_=src[:, :]))
    for rp in range(4):
        fz.dma("LD", y_.dma_start(out=tri_rep[:, rp * 128:(rp + 1) * 128],
                                  in_=d_tri[:, :]))
    fz.dma("LD", y_.dma_start(out=eyer[:, :], in_=d_eyer[:, :]))
    fz.dma("LD", y_.dma_start(out=selpT[:, :], in_=d_selpT[:, :]))
    fz.dma("LD", y_.dma_start(out=selp2[:, :], in_=d_selp2[:, :]))
    fz.dma("LD", y_.dma_start(out=qnc[0:2, 0:1], in_=d_qn[0:1, 0:2]))
    fz.dma("LD", y_.dma_start(out=qnc[0:2, 1:2], in_=d_qw[0:1, 0:2]))

    fz.put("V", v_.memset(recip_tm[:, :], 1.0))
    fz.put("V", v_.memset(Pctx[:, :], 0.0))
    fz.put("V", v_.memset(Pden[:, :], 0.0))
    fz.put("V", v_.memset(PctxBe[:, :], 0.0))
    fz.put("V", v_.memset(PctxBo[:, :], 0.0))
    fz.put("V", v_.memset(PdenBe[:, :], 0.0))
    fz.put("V", v_.memset(PdenBo[:, :], 0.0))
    fz.put("V", v_.memset(one1[:, :], 1.0))
    fz.put("V", v_.memset(onesc2[:, :], 1.0))
    fz.put("V", v_.memset(zpad[:, :], 0.0))
    fz.put("V", v_.memset(ones2[:, :], 0.0))
    fz.fence(["V"], ["V"])
    fz.put("V", v_.memset(ones2[0:64, 0:1], 1.0))
    fz.put("V", v_.memset(ones2[64:128, 1:2], 1.0))

    fz.fence(["LD"], ["V", "S", "T"])
    fz.put("V", v_.tensor_scalar(qnc[0:2, 2:3], qnc[0:2, 0:1], 0.0, 2.0, ALU.max, ALU.mult))
    fz.put("V", v_.tensor_scalar(qnc[0:2, 8:9], qnc[0:2, 1:2], 0.0, 1.0 / 32.0, ALU.max, ALU.mult))
    fz.put("V", v_.tensor_scalar(qnc[0:2, 5:6], qnc[0:2, 0:1], -1.0, None, ALU.mult))
    fz.put("V", v_.tensor_scalar(qnc[0:2, 6:7], qnc[0:2, 0:1], -20.0, None, ALU.add))
    fz.put("V", v_.tensor_scalar(qnc[0:2, 7:8], qnc[0:2, 0:1], 20.0, None, ALU.add))
    fz.fence(["V"], ["S"])
    fz.put("S", s_.activation(out=qnc[0:2, 3:4], in_=qnc[0:2, 2:3], func=AF.Sqrt))
    fz.put("S", s_.copy(recip_tm[:, 0:64], recip_tm[:, 0:64]))
    fz.fence(["V", "S"], ["T"])
    mm(PB[2][:, 0:2], selpT[0:2, :], qnc[0:2, 5:7], True, True)
    mm(PB[3][:, 0:2], selpT[0:2, :], qnc[0:2, 7:9], True, True)
    mm(PB[1][0:1, 0:64], qnc[0:2, 3:4], selp2[0:2, :], True, True)
    fz.fence(["T"], ["V"])
    fz.put("V", v_.tensor_copy(biasq[:, :], PB[2][:, 0:1]))
    fz.put("V", v_.tensor_copy(lo_u[:, :], PB[2][:, 1:2]))
    fz.put("V", v_.tensor_copy(hi_u[:, :], PB[3][:, 0:1]))
    fz.put("V", v_.tensor_copy(sw2c[:, :], PB[3][:, 1:2]))
    fz.put("V", v_.tensor_copy(s2s_row[:, :], PB[1][0:1, 0:64]))
    fz.put("V", v_.tensor_copy(recip_tm[:, 0:64], recip_tm[:, 0:64]))
    fz.fence(["V"], ["T"])
    for qk in range(2):
        for hp in range(8):
            bi = qk * 8 + hp
            pb = PB[2 + bi // 4]
            for half in range(2):
                row = qk * 16 + 2 * hp + half
                mm(pb[0:32, (bi % 4) * 128 + half * 64: (bi % 4) * 128 + half * 64 + 64],
                   eyer[0:1, row * 32:(row + 1) * 32], s2s_row[0:1, :],
                   True, True)
    fz.fence(["T"], ["V"])
    for g4 in range(4):
        fz.put("V", v_.tensor_copy(sel[:, g4 * 512:(g4 + 1) * 512],
                                   PB[2 + g4][0:32, 0:512]))
    fz.fence(["V"], ["T", "S"])

    for jc in range(8):
        buf = xbuf[jc // 4]
        fz.dma("LD", y_.dma_start(out=buf[:, (jc % 4) * 1024:(jc % 4 + 1) * 1024],
                                  in_=d_rhsT[jc * 128:(jc + 1) * 128, :]))
    fz.fence(["LD"], ["V"])
    for jc in range(8):
        src = xbuf[jc // 4][:, (jc % 4) * 1024:(jc % 4 + 1) * 1024]
        fz.put("V", v_.tensor_reduce(stat[:, 0:1], src, AX.X, ALU.max,
                                     apply_absolute_value=True))
        fz.fence(["V"], ["V"])
        fz.put("V", v_.tensor_scalar(stat[:, 1:2], stat[:, 0:1], 1e-12, None, ALU.max))
        fz.fence(["V"], ["V"])
        fz.put("V", v_.tensor_scalar(rs_tm[:, jc:jc + 1], stat[:, 1:2], 1.0 / 127.0,
                                     None, ALU.mult))
        fz.put("V", v_.reciprocal(stat[:, 2:3], stat[:, 1:2]))
        fz.fence(["V"], ["V"])
        fz.put("V", v_.tensor_scalar(stat[:, 3:4], stat[:, 2:3], 127.0, None, ALU.mult))
        dstq = (qf if jc < 4 else kf)[:, (jc % 4) * 1024:(jc % 4 + 1) * 1024]
        for hf in range(2):
            fz.fence(["V"], ["V"])
            fz.put("V", v_.tensor_scalar(u_sb[0][:, :], src[:, hf * 512:(hf + 1) * 512],
                                         stat[:, 3:4], MAGIC, ALU.mult, ALU.add))
            fz.fence(["V"], ["V"])
            fz.put("V", v_.tensor_scalar(dstq[:, hf * 512:(hf + 1) * 512], u_sb[0][:, :],
                                         -MAGIC, None, ALU.add))
    fz.fence(["V"], ["T"])
    for cc in range(8):
        for jc in range(8):
            src = (qf if jc < 4 else kf)[:, (jc % 4) * 1024 + cc * 128:
                                         (jc % 4) * 1024 + cc * 128 + 128]
            tr(PT[:, jc * 128:(jc + 1) * 128], src, idnb[:, :])
        fz.fence(["T"], ["V"])
        fz.put("V", v_.tensor_copy(qrhs[:, cc * 1024:(cc + 1) * 1024], PT[:, :]))
        fz.fence(["V"], ["T"])
    fz.put("V", v_.memset(pat[:, :], 1.0))
    for hf in range(2):
        fz.fence(["V"], ["T"])
        for q in range(4):
            jc = hf * 4 + q
            tr(PB[1][0:1, q * 128:(q + 1) * 128], rs_tm[:, jc:jc + 1], idnf[:, :])
        fz.fence(["T"], ["V"])
        fz.put("V", v_.tensor_copy(blkd[0:1, 0:512], PB[1][0:1, 0:512]))
        fz.fence(["V"], ["T"])
        mm(PB[2][:, :], pat[0:1, :], blkd[0:1, 0:512], True, True)
        fz.fence(["T"], ["V"])
        fz.put("V", v_.tensor_copy(rs_bc[:, hf * 512:(hf + 1) * 512], PB[2][:, :]))
    fz.fence(["V", "GD"], ["T", "S", "Y"])

    def stats1(src):
        fz.put("V", v_.tensor_reduce(stat[:, 16:17], src, AX.X, ALU.add))
        fz.put("S", s_.activation(out=scr_s[:, :], in_=src, func=AF.Square,
                                  accum_out=stat[:, 24:25]))
        fz.fence(["V", "S"], ["V"])
        fz.put("V", v_.tensor_scalar(stat[:, 32:33], stat[:, 16:17], 1.0 / 1024.0,
                                     None, ALU.mult))
        fz.put("V", v_.scalar_tensor_tensor(stat[:, 40:41], stat[:, 16:17],
                                            1.0 / (1024.0 * 1024.0), stat[:, 16:17],
                                            ALU.mult, ALU.mult))
        fz.put("V", v_.tensor_scalar(stat[:, 44:45], stat[:, 24:25], 1.0 / 1024.0,
                                     1e-5, ALU.mult, ALU.add))
        fz.fence(["V"], ["V"])
        fz.put("V", v_.tensor_tensor(stat[:, 48:49], stat[:, 44:45], stat[:, 40:41],
                                     ALU.subtract))
        fz.fence(["V"], ["V"])
        fz.put("V", v_.reciprocal(stat[:, 52:53], stat[:, 48:49]))
        fz.fence(["V"], ["S"])
        fz.put("S", s_.activation(out=stat[:, 0:1], in_=stat[:, 52:53], func=AF.Sqrt))
        fz.fence(["S"], ["V"])
        fz.put("V", v_.scalar_tensor_tensor(stat[:, 8:9], stat[:, 32:33], -1.0,
                                            stat[:, 0:1], ALU.mult, ALU.mult))
        fz.fence(["V"], ["S"])

    chunk_end_V = []
    store_cnt_after = []

    def emit_chunk(ci, src_dram, src_row0, kpass_only, out_row0=None):
        par = ci % 2
        xb = xbuf[par]

        if ci >= 2:
            fz.flush("V")
            y_.wait_ge(fz.sem["V"], chunk_end_V[ci - 2])
            if store_cnt_after[ci - 2] > 0:
                y_.wait_ge(fz.sem["ST"], store_cnt_after[ci - 2])
        src3 = src_dram.rearrange("(n p) c -> p n c", p=128)
        fz.dma("LD", y_.dma_start(
            out=xb[:, :].rearrange("p (n c) -> p n c", c=1024),
            in_=src3[:, src_row0 // 128: src_row0 // 128 + NT, :]))
        fz.fence(["LD"], ["V", "S"])

        def tri_probe(tag):
            if STOP_AFTER == tag:
                fz.fence(["T", "V", "S"], ["V"])
                fz.put("V", v_.tensor_copy(xb[:, 0:128], tri[:, :]))
                fz.fence(["V"], ["Y", "T", "S"])
                return True
            return False

        if STOP_AFTER == "dump_tri":
            fz.fence(["T", "V", "S"], ["V"])
            fz.put("V", v_.tensor_copy(xb[:, 0:128], tri[:, :]))
            fz.put("V", v_.tensor_copy(xb[:, 128:129], biasq[:, :]))
            fz.put("V", v_.tensor_copy(xb[:, 132:164], den_sb[:, 0:32]))
            fz.fence(["V"], ["Y", "T", "S"])
            return
        for j in range(NT):
            stats1(xb[:, j * 1024:(j + 1) * 1024])
            xl = xln_tm[:, 0:1024]
            fz.put("S", s_.activation(out=xl, in_=xb[:, j * 1024:(j + 1) * 1024],
                                      func=AF.Identity, bias=stat[:, 8:9],
                                      scale=stat[:, 0:1]))
            fz.fence(["S"], ["T"])
            for kc in range(8):
                tr(PT[:, kc * 128:(kc + 1) * 128], xl[:, kc * 128:(kc + 1) * 128],
                   idnb[:, :])
            fz.fence(["T"], ["V"])
            dst = xln_fm[:, :].rearrange("p (k t) -> p k t", t=512)[:, :, j * 128:(j + 1) * 128]
            fz.put("V", v_.tensor_copy(dst, PT[:, :].rearrange("p (k t) -> p k t", t=128)))
            fz.fence(["V"], ["T", "S"])

        if STOP_AFTER == "ln1" or tri_probe("tri_ln1"):
            return
        ccs = list(range(8, 16)) if kpass_only else list(range(16))
        sS = {}

        def norm_mms(i):
            cc = ccs[i]
            hp = cc % 8
            qk = 0 if cc < 8 else 1
            qs = attn_fm[:, 512:1024] if i % 2 == 0 else attn_fm[:, 0:512]
            t_.wait_ge(fz.sem["S"], sS[i])
            for j in range(NT):
                col = j * 32 + qk * 16 + 2 * hp
                mm(PB[6][:, col:col + 2], qs[:, j * 128:(j + 1) * 128], ones2[:, :],
                   True, True)

        for i, cc in enumerate(ccs):
            pb = PB[i % 2]
            if i >= 2:
                t_.wait_ge(fz.sem["S"], sS[i - 2])
            for kc in range(8):
                mm(pb[:, :], wqkv[:, kc * 3072 + cc * 128: kc * 3072 + (cc + 1) * 128],
                   xln_fm[:, kc * 512:(kc + 1) * 512], kc == 0, kc == 7)
            fz.fence(["T"], ["S"])
            hp = cc % 8
            dst = qfm if cc < 8 else kfm
            fz.put("S", s_.activation(out=dst[:, hp * 512:(hp + 1) * 512], in_=pb[:, :],
                                      func=AF.Identity, bias=bqkv_cm[:, cc:cc + 1]))
            fz.put("S", s_.activation(out=(attn_fm[:, 512:1024] if i % 2 == 0
                                           else attn_fm[:, 0:512]),
                                      in_=pb[:, :], func=AF.Square,
                                      bias=bqkv_cm[:, cc:cc + 1]))
            fz.flush("S")
            sS[i] = fz.cnt["S"]
            if i >= 1:
                norm_mms(i - 1)
        norm_mms(len(ccs) - 1)
        if STOP_AFTER == "qkv" or tri_probe("tri_qkv"):
            return
        for j in range(NT):
            for half in range(2):
                pb = PB[2 + half]
                for kc in range(8):
                    mm(pb[:, :],
                       xln_fm[:, kc * 512 + j * 128: kc * 512 + (j + 1) * 128],
                       wqkv[:, kc * 3072 + 2048 + half * 512:
                            kc * 3072 + 2048 + (half + 1) * 512],
                       kc == 0, kc == 7)
            fz.fence(["T"], ["V"])
            for half in range(2):
                fz.put("V", v_.tensor_tensor(
                    vtm[:, j * 1024 + half * 512: j * 1024 + (half + 1) * 512],
                    PB[2 + half][:, :], bv_bc[:, half * 512:(half + 1) * 512], ALU.add))
            fz.fence(["V"], ["T"])

        if STOP_AFTER == "v":
            return
        fz.fence(["T"], ["V"])
        qks = [1] if kpass_only else [0, 1]
        slots = [(j, qk) for j in range(NT) for qk in qks]
        for i, (j, qk) in enumerate(slots):
            sl = slice(j * 32 + qk * 16, j * 32 + qk * 16 + 16)
            fz.put("V", v_.tensor_scalar(scr_s[:, i * 16:(i + 1) * 16], PB[6][:, sl],
                                         1e-24, None, ALU.max))
        fz.fence(["V"], ["V"])
        for i, (j, qk) in enumerate(slots):
            sl = slice(j * 32 + qk * 16, j * 32 + qk * 16 + 16)
            fz.put("V", v_.reciprocal(recip_tm[:, sl], scr_s[:, i * 16:(i + 1) * 16]))
        fz.fence(["V"], ["S"])
        for j in range(NT):
            for qk in qks:
                sl = slice(j * 32 + qk * 16, j * 32 + qk * 16 + 16)
                fz.put("S", s_.activation(out=recip_tm[:, sl], in_=recip_tm[:, sl],
                                          func=AF.Sqrt))
        fz.fence(["S"], ["T"])
        for j in range(NT):
            tr(PB[5][0:32, j * 128:(j + 1) * 128], recip_tm[:, j * 32:(j + 1) * 32],
               idnf[:, :])
        fz.fence(["T"], ["V"])
        fz.put("V", v_.tensor_copy(recip_fm[:, :], PB[5][0:32, :]))
        fz.fence(["V"], ["T"])

        if STOP_AFTER == "norms" or tri_probe("tri_norms"):
            return
        sides = [("k", kfm, kf)] if kpass_only else [("q", qfm, qf), ("k", kfm, kf)]
        for sname, src_fm, dst_f in sides:
            qk = 0 if sname == "q" else 1
            for hp in range(8):
                p2 = hp % 2
                for half in range(2):
                    h = 2 * hp + half
                    b = half * 64
                    mm(PB[4][b:b + 64, :], omega[b:b + 64, h * 64:(h + 1) * 64],
                       src_fm[b:b + 64, hp * 512:(hp + 1) * 512], True, True)
                mm(PB[5][:, :], sel[:, (qk * 8 + hp) * 128:(qk * 8 + hp + 1) * 128],
                   recip_fm[:, :], True, True)
                fz.fence(["T"], ["S"])
                fz.put("S", s_.copy(recipb_sb[p2][:, :], PB[5][:, :]))
                fz.fence(["S"], ["V"])
                fz.put("V", v_.tensor_tensor(u_sb[p2][:, :], PB[4][:, :],
                                             recipb_sb[p2][:, :], ALU.mult))
                fz.fence(["V"], ["V"])
                fz.put("V", v_.tensor_scalar(u_sb[p2][:, :], u_sb[p2][:, :],
                                             lo_u[:, :], hi_u[:, :], ALU.max, ALU.min))
                fz.fence(["V"], ["S", "T"])
                fz.put("S", s_.activation(out=dst_f[:, hp * 512:(hp + 1) * 512],
                                          in_=u_sb[p2][:, :], func=AF.Exp,
                                          bias=biasq[:, :]))
            fz.fence(["S"], ["V"])
            if sname == "k":
                for hp in range(8):
                    fz.put("V", v_.tensor_scalar(dst_f[:, hp * 512:(hp + 1) * 512],
                                                 dst_f[:, hp * 512:(hp + 1) * 512],
                                                 sw2c[:, :], None, ALU.mult))
        fz.fence(["V", "S"], ["T"])

        if STOP_AFTER == "features" or tri_probe("tri_feat"):
            return
        for grp in range(4):
            for pp in range(2):
                hp = grp * 2 + pp
                for j in range(NT):
                    tr(PT[:, pp * 512 + j * 128: pp * 512 + (j + 1) * 128],
                       kf[:, hp * 512 + j * 128: hp * 512 + (j + 1) * 128],
                       idnb[:, :])
            fz.fence(["T"], ["V"])
            fz.put("V", v_.tensor_copy(kftm[:, grp * 1024:(grp + 1) * 1024], PT[:, :]))
            fz.fence(["V"], ["T"])

        if STOP_AFTER == "kftm" or tri_probe("tri_kftm"):
            return
        if STOP_AFTER == "dump_v" and not kpass_only:
            fz.fence(["T", "S"], ["V"])
            fz.put("V", v_.tensor_copy(xb[:, 0:4096], vtm[:, 0:4096]))
            fz.fence(["V"], ["Y", "T", "S"])
            return
        if STOP_AFTER == "dump_P" and not kpass_only:
            fz.fence(["T", "S"], ["V"])
            fz.put("V", v_.tensor_copy(xb[:, 0:512], Pctx[:, 0:512]))
            fz.put("V", v_.tensor_copy(xb[:, 512:520], Pden[:, 0:8]))
            fz.put("V", v_.tensor_copy(xb[:, 1024:2048], kftm[:, 0:1024]))
            fz.fence(["V"], ["Y", "T", "S"])
            return
        if (not kpass_only) and STOP_AFTER in ("scan_kv", "scan_kv_even", "pstate"):
            return
        if kpass_only:
            heads = list(range(16))
            if STOP_AFTER == "scan_kv_even":
                heads = [h for h in heads if h % 2 == 0]
            for h in heads:
                b = (h % 2) * 64
                for j in range(NT):
                    st0 = (h in (0, 1) and j == 0) or (STOP_AFTER == "scan_kv_even" and h == 0 and j == 0)
                    sp0 = (h in (14, 15) and j == NT - 1) or (STOP_AFTER == "scan_kv_even" and h == 14 and j == NT - 1)
                    kslc = kftm[:, (h // 2) * 512 + j * 128 + (h % 2) * 64:
                                (h // 2) * 512 + j * 128 + (h % 2) * 64 + 64]
                    mm(PB[4][b:b + 64, 64 * (h // 2): 64 * (h // 2) + 64],
                       kslc,
                       vtm[:, j * 1024 + h * 64: j * 1024 + (h + 1) * 64], st0, sp0)
                    mm(PB[5][b:b + 64, 2 * (h // 2): 2 * (h // 2) + 2],
                       kslc, onesc2[:, :], st0, sp0)
            fz.fence(["T"], ["V"])
            if STOP_AFTER in ("scan_kv", "scan_kv_even"):
                return
            fz.put("V", v_.tensor_tensor(Pctx[:, :], Pctx[:, :], PB[4][:, :], ALU.add))
            fz.put("V", v_.tensor_tensor(Pden[:, :], Pden[:, :], PB[5][:, 0:16], ALU.add))
            fz.fence(["V"], ["T"])
        else:
            for j in range(NT):
                if STOP_AFTER in ("scan1", "dump_den") and j > 0:
                    break
                fz.fence(["T", "V"], ["V"])
                fz.put("V", v_.tensor_copy(PctxBe[0:64, :], Pctx[0:64, :]))
                fz.put("V", v_.tensor_copy(PctxBo[64:128, :], Pctx[64:128, :]))
                fz.put("V", v_.tensor_copy(PdenBe[0:64, :], Pden[0:64, :]))
                fz.put("V", v_.tensor_copy(PdenBo[64:128, :], Pden[64:128, :]))
                fz.fence(["V"], ["T"])
                sc_bank = [PB[2], PB[3], PB[4], PB[6]]
                for half in range(2):
                    for h in range(half, 16, 2):
                        hp = h // 2
                        b = half * 64
                        qfs = qf[b:b + 64, hp * 512 + j * 128: hp * 512 + (j + 1) * 128]
                        kfs = kf[b:b + 64, hp * 512 + j * 128: hp * 512 + (j + 1) * 128]
                        mm(sc_bank[h // 4][:, (h % 4) * 128:(h % 4 + 1) * 128],
                           kfs, qfs, True, True)
                    if half == 0:
                        fz.flush("T")
                fz.fence(["T"], ["V"])
                if tri_probe("tri_pA"):
                    return
                fz.put("V", v_.tensor_tensor(smb_all[0][:, 0:512], PB[2][:, :],
                                             tri_rep[:, :], ALU.mult))
                fz.put("V", v_.tensor_tensor(smb_all[0][:, 512:1024], PB[3][:, :],
                                             tri_rep[:, :], ALU.mult))
                fz.put("V", v_.tensor_tensor(smb_all[1][:, 0:512], PB[4][:, :],
                                             tri_rep[:, :], ALU.mult))
                fz.put("V", v_.tensor_tensor(smb_all[1][:, 512:1024], PB[6][:, :],
                                             tri_rep[:, :], ALU.mult))
                fz.fence(["V"], ["T"])
                if tri_probe("tri_pB"):
                    return
                for h in range(16):
                    hp, half = h // 2, h % 2
                    cb = PB[2 + h // 8]
                    pce = PctxBe if half == 0 else PctxBo
                    pde = PdenBe if half == 0 else PdenBo
                    qfs_full = qf[:, hp * 512 + j * 128: hp * 512 + (j + 1) * 128]
                    smbs = smb_all[h // 8][:, (h % 8) * 128:(h % 8 + 1) * 128]
                    mm(cb[:, (h % 8) * 64:(h % 8) * 64 + 64], qfs_full,
                       pce[:, 64 * (h // 2): 64 * (h // 2) + 64], True, False)
                    mm(cb[:, (h % 8) * 64:(h % 8) * 64 + 64], smbs,
                       vtm[:, j * 1024 + h * 64: j * 1024 + (h + 1) * 64],
                       False, True)
                    mm(PB[5][:, 16 + 2 * h: 18 + 2 * h], qfs_full,
                       pde[:, 2 * (h // 2): 2 * (h // 2) + 2], True, False)
                    mm(PB[5][:, 16 + 2 * h: 18 + 2 * h], smbs,
                       onesc2[:, :], False, True)
                for h in range(16):
                    b = (h % 2) * 64
                    kslc = kftm[:, (h // 2) * 512 + j * 128 + (h % 2) * 64:
                                (h // 2) * 512 + j * 128 + (h % 2) * 64 + 64]
                    mm(PB[4][b:b + 64, 64 * (h // 2): 64 * (h // 2) + 64],
                       kslc,
                       vtm[:, j * 1024 + h * 64: j * 1024 + (h + 1) * 64],
                       True, True)
                    mm(PB[5][b:b + 64, 2 * (h // 2): 2 * (h // 2) + 2],
                       kslc, onesc2[:, :], True, True)
                fz.fence(["T"], ["V"])
                if tri_probe("tri_pC"):
                    return
                fz.put("V", v_.tensor_scalar(den_sb[:, 32:64], PB[5][:, 16:48], 1e-6,
                                             None, ALU.max))
                fz.fence(["V"], ["V"])
                fz.put("V", v_.reciprocal(den_sb[:, 0:32], den_sb[:, 32:64]))
                fz.fence(["V"], ["V"])
                if STOP_AFTER == "dump_den" and j == 0:
                    fz.put("V", v_.tensor_copy(xb[:, 0:32], den_sb[:, 0:32]))
                    fz.put("V", v_.tensor_copy(xb[:, 64:576], PB[2][:, :]))
                    fz.put("V", v_.tensor_copy(xb[:, 1024:2048], smb_all[0][:, :]))
                    fz.put("V", v_.tensor_copy(xb[:, 2048:2064], PB[5][:, 0:16]))
                    fz.put("V", v_.tensor_copy(xb[:, 2176:2304], tri[:, :]))
                    fz.put("V", v_.tensor_copy(xb[:, 2304:2432], sc_bank[0][:, 0:128]))
                    fz.fence(["V"], ["Y", "T", "S"])
                for h in range(16):
                    fz.put("V", v_.tensor_scalar(
                        attn_tm[:, j * 1024 + h * 64: j * 1024 + (h + 1) * 64],
                        PB[2 + h // 8][:, (h % 8) * 64:(h % 8) * 64 + 64],
                        den_sb[:, 2 * h:2 * h + 1], None, ALU.mult))
                fz.put("V", v_.tensor_tensor(Pctx[:, :], Pctx[:, :], PB[4][:, :],
                                             ALU.add))
                fz.put("V", v_.tensor_tensor(Pden[:, :], Pden[:, :], PB[5][:, 0:16],
                                             ALU.add))
                fz.fence(["V"], ["T"])

            for j in range(NT):
                if STOP_AFTER == "dump_cols":
                    fz.put("V", v_.tensor_copy(xb[:, j * 1024 + 0: j * 1024 + 1], biasq[:, :]))
                    fz.put("V", v_.tensor_copy(xb[:, j * 1024 + 1: j * 1024 + 2], lo_u[:, :]))
                    fz.put("V", v_.tensor_copy(xb[:, j * 1024 + 2: j * 1024 + 3], hi_u[:, :]))
                    fz.put("V", v_.tensor_copy(xb[:, j * 1024 + 3: j * 1024 + 4], sw2c[:, :]))
                    fz.put("V", v_.tensor_copy(xb[0:2, j * 1024 + 4: j * 1024 + 8], qnc[0:2, 0:4]))
                    fz.put("V", v_.tensor_copy(xb[0:1, j * 1024 + 8: j * 1024 + 520], recipb_sb[0][0:1, :]))
                    fz.fence(["V"], ["Y", "T", "S"])
                    continue
                if STOP_AFTER == "dump_attn":
                    fz.put("V", v_.tensor_copy(xb[:, j * 1024:(j + 1) * 1024],
                                               attn_tm[:, j * 1024:(j + 1) * 1024]))
                    fz.fence(["V"], ["Y", "T", "S"])
                    continue
                if STOP_AFTER == "dump_qf":
                    fz.put("V", v_.tensor_copy(xb[:, j * 1024:(j + 1) * 1024],
                                               qf[:, j * 1024:(j + 1) * 1024]))
                    fz.fence(["V"], ["Y", "T", "S"])
                    continue
                if STOP_AFTER == "dump_kf":
                    fz.put("V", v_.tensor_copy(xb[:, j * 1024:(j + 1) * 1024],
                                               kf[:, j * 1024:(j + 1) * 1024]))
                    fz.fence(["V"], ["Y", "T", "S"])
                    continue
                for cc in range(8):
                    tr(PT[:, cc * 128:(cc + 1) * 128],
                       attn_tm[:, j * 1024 + cc * 128: j * 1024 + cc * 128 + 128],
                       idnb[:, :])
                fz.fence(["T"], ["V"])
                fz.put("V", v_.tensor_copy(attn_fm[:, :], PT[:, :]))
                fz.fence(["V"], ["T"])
                for half in range(2):
                    pb = PB[half]
                    for kc in range(8):
                        mm(pb[:, :], attn_fm[:, kc * 128:(kc + 1) * 128],
                           woutS[:, kc * 1024 + half * 512:
                                 kc * 1024 + (half + 1) * 512], kc == 0, kc == 7)
                fz.fence(["T"], ["V"])
                for half in range(2):
                    fz.put("V", v_.tensor_tensor(
                        ffn_sb[:, half * 512:(half + 1) * 512], PB[half][:, :],
                        bout_bc[:, half * 512:(half + 1) * 512], ALU.add))
                if STOP_AFTER == "dump_aproj":
                    fz.put("V", v_.tensor_copy(xb[:, j * 1024:(j + 1) * 1024], ffn_sb[:, :]))
                    fz.fence(["V"], ["Y", "T", "S"])
                    continue
                fz.fence(["V"], ["V", "S"])
                stats1(ffn_sb[:, :])
                fz.put("S", s_.activation(out=scr_s[:, :], in_=ffn_sb[:, :],
                                          func=AF.Identity, bias=stat[:, 8:9],
                                          scale=stat[:, 0:1]))
                fz.fence(["S"], ["V"])
                fz.put("V", v_.tensor_tensor(xb[:, j * 1024:(j + 1) * 1024],
                                             scr_s[:, :],
                                             xb[:, j * 1024:(j + 1) * 1024], ALU.add))
                if STOP_AFTER == "dump_x2":
                    fz.fence(["V"], ["Y", "T", "S"])
                    continue
                if STOP_AFTER == "ep_out":
                    continue
                fz.fence(["V"], ["V", "S"])
                stats1(xb[:, j * 1024:(j + 1) * 1024])
                fz.put("S", s_.activation(out=ln3b[:, :],
                                          in_=xb[:, j * 1024:(j + 1) * 1024],
                                          func=AF.Identity, bias=stat[:, 8:9],
                                          scale=stat[:, 0:1]))
                fz.fence(["S"], ["V"])
                fz.put("V", v_.tensor_reduce(stat[:, 16:17], ln3b[:, :], AX.X, ALU.max,
                                             apply_absolute_value=True))
                fz.fence(["V"], ["V"])
                fz.put("V", v_.tensor_scalar(stat[:, 17:18], stat[:, 16:17], 1e-12,
                                             None, ALU.max))
                fz.fence(["V"], ["V"])
                fz.put("V", v_.tensor_scalar(stat[:, 18:19], stat[:, 17:18],
                                             1.0 / 127.0, None, ALU.mult))
                fz.put("V", v_.reciprocal(stat[:, 19:20], stat[:, 17:18]))
                fz.fence(["V"], ["V"])
                fz.put("V", v_.tensor_scalar(stat[:, 20:21], stat[:, 19:20], 127.0,
                                             None, ALU.mult))
                fz.fence(["V"], ["V"])
                fz.put("V", v_.tensor_scalar(scr_s[:, :], ln3b[:, :], stat[:, 20:21],
                                             MAGIC, ALU.mult, ALU.add))
                fz.fence(["V"], ["V"])
                fz.put("V", v_.tensor_scalar(qlhs_tm()[:, :], scr_s[:, :], -MAGIC,
                                             None, ALU.add))
                fz.fence(["V"], ["T"])
                for cc in range(8):
                    tr(PT[:, cc * 128:(cc + 1) * 128],
                       qlhs_tm()[:, cc * 128:(cc + 1) * 128], idnb[:, :])
                fz.fence(["T"], ["V"])
                fz.put("V", v_.tensor_copy(qlhs_fm()[:, :], PT[:, :]))
                fz.fence(["V"], ["T"])
                for half in range(2):
                    pb = PB[half]
                    for kc in range(8):
                        mm(pb[:, :], qlhs_fm()[:, kc * 128:(kc + 1) * 128],
                           qrhs[:, kc * 1024 + half * 512:
                                kc * 1024 + (half + 1) * 512], kc == 0, kc == 7)
                fz.fence(["T"], ["V"])
                for half in range(2):
                    fz.put("V", v_.scalar_tensor_tensor(
                        ffn_sb[:, half * 512:(half + 1) * 512], PB[half][:, :],
                        stat[:, 18:19], rs_bc[:, half * 512:(half + 1) * 512],
                        ALU.mult, ALU.mult))
                if STOP_AFTER == "ep_ffn":
                    continue
                if STOP_AFTER == "dump_ffn":
                    fz.put("V", v_.tensor_copy(xb[:, j * 1024:(j + 1) * 1024], ffn_sb[:, :]))
                    fz.fence(["V"], ["Y", "T", "S"])
                    continue
                if STOP_AFTER == "dump_rs":
                    fz.put("V", v_.tensor_copy(xb[:, j * 1024:(j + 1) * 1024], rs_bc[:, :]))
                    fz.fence(["V"], ["Y", "T", "S"])
                    continue
                if STOP_AFTER == "dump_ln3":
                    fz.put("V", v_.tensor_copy(xb[:, j * 1024:(j + 1) * 1024], ln3b[:, :]))
                    fz.fence(["V"], ["Y", "T", "S"])
                    continue
                if STOP_AFTER == "dump_qlhstm":
                    fz.put("V", v_.tensor_copy(xb[:, j * 1024:(j + 1) * 1024], qlhs_tm()[:, :]))
                    fz.fence(["V"], ["Y", "T", "S"])
                    continue
                if STOP_AFTER == "dump_stat":
                    fz.put("V", v_.tensor_copy(xb[:, j * 1024: j * 1024 + 64], stat[:, :]))
                    fz.fence(["V"], ["Y", "T", "S"])
                    continue
                if STOP_AFTER == "dump_qlhs":
                    fz.put("V", v_.tensor_copy(xb[:, j * 1024:(j + 1) * 1024], qlhs_fm()[:, :]))
                    fz.fence(["V"], ["Y", "T", "S"])
                    continue
                fz.fence(["V"], ["V", "S"])
                stats1(ffn_sb[:, :])
                fz.put("S", s_.activation(out=scr_s[:, :], in_=ffn_sb[:, :],
                                          func=AF.Identity, bias=stat[:, 8:9],
                                          scale=stat[:, 0:1]))
                fz.fence(["S"], ["V"])
                fz.put("V", v_.tensor_tensor(xb[:, j * 1024:(j + 1) * 1024],
                                             scr_s[:, :],
                                             xb[:, j * 1024:(j + 1) * 1024], ALU.add))
                fz.fence(["V"], ["Y", "T", "S"])
            dst3 = d_y.rearrange("(n p) c -> p n c", p=128)
            fz.dma("ST", y_.dma_start(
                out=dst3[:, out_row0 // 128: out_row0 // 128 + NT, :],
                in_=xb[:, :].rearrange("p (n c) -> p n c", c=1024)))

        fz.flush("V")
        chunk_end_V.append(fz.cnt["V"])
        store_cnt_after.append(fz.cnt["ST"])

    if STOP_AFTER != "setup":
        for ci in range(NCH_PRE):
            emit_chunk(ci, d_xpre, ci * CH, True)
        fz.fence(["T", "V", "S"], ["V"])
        fz.put("V", v_.tensor_scalar(Pctx[:, :], Pctx[:, :], gate[:, :], None, ALU.mult))
        fz.put("V", v_.tensor_scalar(Pden[:, :], Pden[:, :], gate[:, :], None, ALU.mult))
        fz.fence(["V"], ["T", "S"])
        for ci in range(NCH_OWN):
            emit_chunk(NCH_PRE + ci, d_xown, ci * CH, False, out_row0=ci * CH)
    fz.fence(["T", "V", "S"], ["Y"])
    if fz.cnt["ST"] == 0:
        lastbuf = xbuf[(NCH_PRE + NCH_OWN - 1) % 2]
        dst3 = d_y.rearrange("(n p) c -> p n c", p=128)
        fz.dma("ST", y_.dma_start(out=dst3[:, 0:NT, :],
                                  in_=lastbuf[:, :].rearrange("p (n c) -> p n c", c=1024)))
    fz.fence(["T", "V", "S"], ["G"])
    g_.wait_ge(fz.sem["ST"], fz.cnt["ST"])
    g_.wait_ge(fz.sem["LD"], fz.cnt["LD"])

    es.close()
    fz.es.close()
    return nc


_NC_CACHE = {}


def _get_program():
    if "nc" not in _NC_CACHE:
        _NC_CACHE["nc"] = build_program()
    return _NC_CACHE["nc"]


def kernel(L0_attn_x, L0_attn_Wqkv, L0_attn_bqkv, L0_attn_Wout, L0_attn_bout,
           L0_attn_omega, L0_attn_quad_nodes, L0_attn_quad_weights, L0_ffn_rhs):
    x = np.ascontiguousarray(np.asarray(L0_attn_x, dtype=np.float32))
    wqkv = np.ascontiguousarray(np.asarray(L0_attn_Wqkv, dtype=np.float32))
    bqkv = np.asarray(L0_attn_bqkv, dtype=np.float32)
    wout = np.ascontiguousarray(np.asarray(L0_attn_Wout, dtype=np.float32))
    bout = np.asarray(L0_attn_bout, dtype=np.float32)
    omega = np.asarray(L0_attn_omega, dtype=np.float32)
    qn = np.asarray(L0_attn_quad_nodes, dtype=np.float32)
    qw = np.asarray(L0_attn_quad_weights, dtype=np.float32)
    rhs = np.ascontiguousarray(np.asarray(L0_ffn_rhs, dtype=np.float32))

    rhsT = np.ascontiguousarray(rhs.T)
    bqkv_cm = np.ascontiguousarray(bqkv[:2048].reshape(16, 128).T)
    bv_bc = np.ascontiguousarray(np.broadcast_to(bqkv[2048:], (128, 1024)))
    bout_bc = np.ascontiguousarray(np.broadcast_to(bout, (128, 1024)))
    om = omega.transpose(2, 1, 0, 3).reshape(64, 1024)
    omega_h = np.ascontiguousarray(np.concatenate([om, om], axis=0))
    idn_bf = np.eye(128, dtype=np.float32).astype(ml_dtypes.bfloat16)
    idn_dup = np.ascontiguousarray(
        np.concatenate([np.eye(64), np.eye(64)], axis=0)).astype(ml_dtypes.bfloat16)
    idn_f32 = np.eye(128, dtype=np.float32)
    trimask = np.ascontiguousarray(np.triu(np.ones((128, 128)))).astype(ml_dtypes.bfloat16)
    ones_col = np.ones((128, 1), dtype=np.float32).astype(ml_dtypes.bfloat16)
    eye_rows = np.ascontiguousarray(np.eye(32, dtype=np.float32).reshape(1, 1024))
    selpatT = np.zeros((2, 128), dtype=np.float32)
    for jj in range(128):
        selpatT[(jj // 32) % 2, jj] = 1.0
    selpat2 = np.zeros((2, 64), dtype=np.float32)
    for jj in range(64):
        selpat2[jj // 32, jj] = 1.0

    shared = dict(wqkv=wqkv, wout=wout, rhsT=rhsT, bqkv_cm=bqkv_cm,
                  bv_bcast=bv_bc, bout_bcast=bout_bc, omega_h=omega_h,
                  quad_n=qn.reshape(1, 2), quad_w=qw.reshape(1, 2),
                  idn_bf=idn_bf, idn_dup=idn_dup, idn_f32=idn_f32,
                  trimask=trimask, ones_col=ones_col, eye_rows=eye_rows,
                  selpatT=selpatT, selpat2=selpat2)

    in_maps = []
    for c in range(NC_):
        b, half = c // 2, c % 2
        xb = x[b]
        m = dict(shared)
        m["xpre"] = np.ascontiguousarray(xb[0:TPRE])
        m["xown"] = np.ascontiguousarray(xb[half * TOWN:(half + 1) * TOWN])
        m["gate"] = np.full((128, 1), float(half), dtype=np.float32)
        in_maps.append(m)

    nc = _get_program()
    res = run_bass_kernel_spmd(nc, in_maps, core_ids=list(range(NC_)))

    out = np.empty((B, T, C), dtype=np.float32)
    for c in range(NC_):
        b, half = c // 2, c % 2
        out[b, half * TOWN:(half + 1) * TOWN] = res.results[c]["y"]
    return out
